# version 30
# baseline (speedup 1.0000x reference)
import os
import numpy as np
import ml_dtypes
import concourse.bass as bass
import concourse.mybir as mybir
from concourse.bass_utils import run_bass_kernel_spmd

F32 = mybir.dt.float32
BF16 = mybir.dt.bfloat16
AF = mybir.ActivationFunctionType
ALU = mybir.AluOpType
AX = mybir.AxisListType

NCORE = 8
SEQ = 4096
D = 1024
H = 16
DH = 64
NT = 32
SUP = 4
NS = 8
KC = 8
EPS = 1e-5
EXPB = -8.0
MASKV = -30000.0

X_RING = 2
AO_RING = 2
AOT_RING = 4
XR_RING = 2

CHUNK_ORDER = [0, 4, 1, 5, 2, 6, 3, 7]
WQK_COVER = {}
WQK_CNT = {0: 0, 1: 0, 2: 0, 3: 0}
KLIM = int(os.environ.get("KLIM", str(NS + 3)))
KNO = set(os.environ.get("KNO", "").split(",")) - {""}
KDBG = os.environ.get("KDBG", "0") == "1"
KDRAIN = os.environ.get("KDRAIN", "1") == "1"
NSTORE = sum(1 for sg in range(KLIM) for t in range(4) if 0 <= sg - 3 < NS)


def _quad_perm():
    perm = np.zeros(1024, dtype=np.int64)
    for c in range(8):
        half = c // 4
        for p in range(128):
            head = 4 * (c % 4) + p // 32
            d = (p % 32) + 32 * half
            perm[c * 128 + p] = head * DH + d
    return perm


def _host_constants():
    cst = {}
    inv = 10000.0 ** (-(np.arange(32) / 32.0))
    t = np.arange(128)
    idx = np.arange(128) % 32
    phq = (128.0 + t)[None, :] * inv[idx][:, None]

    def tile4(a):
        return np.ascontiguousarray(np.tile(a, (1, SUP))).astype(ml_dtypes.bfloat16)

    cst["c_cosq"] = np.ascontiguousarray(np.cos(phq) * 0.125).astype(
        ml_dtypes.bfloat16)
    cst["c_sinq"] = np.ascontiguousarray(np.sin(phq) * 0.125).astype(
        ml_dtypes.bfloat16)
    j = np.arange(128)[:, None]
    q = np.arange(128)[None, :]
    m1 = np.where(j < q, MASKV, 0.0)
    m2 = np.where(j > q, MASKV, 0.0)
    cst["c_m12"] = np.ascontiguousarray(
        np.concatenate([m1, m2, m1, m2], axis=1)).astype(ml_dtypes.bfloat16)
    cst["c_id"] = np.eye(128, dtype=ml_dtypes.bfloat16)
    cst["c_ones"] = np.ones((1, 128), dtype=ml_dtypes.bfloat16)
    delta = 128.0 * inv[idx]
    cst["c_dl"] = np.stack([np.cos(delta), np.sin(delta)], axis=1).astype(np.float32)
    grid = (2.0 * np.pi / D) * ((np.arange(D)[:, None] * np.arange(D)[None, :]) % D)
    cst["c_dftC"] = np.cos(grid).astype(ml_dtypes.bfloat16)
    cst["c_dftS"] = np.sin(grid).astype(ml_dtypes.bfloat16)
    return cst


CONST_NAMES = ["c_cosq", "c_sinq", "c_m12", "c_id", "c_ones", "c_dl"]
_SHAPES = {"c_id": [128, 128], "c_ones": [1, 128], "c_dl": [128, 2],
           "c_cosq": [128, 128], "c_sinq": [128, 128]}


def build_nc():
    nc = bass.Bass("TRN2", target_bir_lowering=False)

    xin = nc.dram_tensor("xin", [SEQ, D], F32, kind="ExternalInput").ap()
    wqk = nc.dram_tensor("wqk", [128, 128, 128], BF16, kind="ExternalInput").ap()
    wv = nc.dram_tensor("wv", [D, D], BF16, kind="ExternalInput").ap()
    wo = nc.dram_tensor("wo", [D, D], BF16, kind="ExternalInput").ap()
    wfr = nc.dram_tensor("wfr", [D, D], BF16, kind="ExternalInput").ap()
    wfi = nc.dram_tensor("wfi", [D, D], BF16, kind="ExternalInput").ap()
    fbias = nc.dram_tensor("fbias", [1, D], BF16, kind="ExternalInput").ap()
    dsub = nc.dram_tensor("dsub", [3, D, 128], BF16, kind="ExternalInput").ap()
    c_in = {}
    for nm in CONST_NAMES:
        shp = _SHAPES.get(nm, [128, 512])
        dt = F32 if nm == "c_dl" else BF16
        c_in[nm] = nc.dram_tensor(nm, shp, dt, kind="ExternalInput").ap()
    c_dftC = nc.dram_tensor("c_dftC", [D, D], BF16, kind="ExternalInput").ap()
    c_dftS = nc.dram_tensor("c_dftS", [D, D], BF16, kind="ExternalInput").ap()
    yout = nc.dram_tensor("yout", [SEQ, D], F32, kind="ExternalOutput").ap()
    dbg = {}
    if KDBG:
        for nm, shp in (("d_ln1t", [128, KC, 512]), ("d_qkst", [128, 2, 512]),
                        ("d_qro", [128, KC, 512]), ("d_khi", [128, KC, 512]),
                        ("d_klo", [128, KC, 384]), ("d_abuf", [128, 4, 2, 128]),
                        ("d_ao", [128, D])):
            dbg[nm] = nc.dram_tensor(nm, shp, BF16, kind="ExternalOutput").ap()

    ccin = nc.dram_tensor("ccin", [128, D], F32, kind="Internal")
    ccout = nc.dram_tensor("ccout", [D, D], F32, kind="Internal")

    A = nc.alloc_sbuf_tensor
    wv_sb = A("wv_sb", [128, KC, D], BF16)
    wo_sb = A("wo_sb", [128, KC, D], BF16)
    m_sb = A("m_sb", [128, KC, D], BF16)
    wqk_flat = A("wqk_sb", [128, 32, 128], BF16)
    wqk_sb = [wqk_flat[:][:, 8 * i:8 * (i + 1), :] for i in range(4)]
    x_sb = [A(f"x_sb{i}", [128, D], BF16) for i in range(X_RING)]
    ln1T = [A(f"ln1T{i}", [128, KC, 512], BF16, is_transpose_dest=True)
            for i in range(2)]
    qro = [A(f"qro{i}", [128, KC, 512], BF16) for i in range(2)]
    khi = [A(f"khi{i}", [128, KC, 512], BF16) for i in range(2)]
    klo = [A(f"klo{i}", [128, KC, 512], BF16) for i in range(2)]
    vpr = [A(f"vpr{i}", [128, SUP, H * 65], BF16) for i in range(2)]
    vprev = [A(f"vprev{i}", [128, H * 65], BF16) for i in range(2)]
    abuf = [A(f"abuf{i}", [128, 4, 512], BF16) for i in range(2)]
    ao = [A(f"ao{i}", [128, D], BF16) for i in range(AO_RING)]
    aoT = [A(f"aoT{i}", [128, KC, 128], BF16, is_transpose_dest=True)
           for i in range(AOT_RING)]
    xr = [A(f"xr{i}", [128, D], F32) for i in range(XR_RING)]
    x1 = [A(f"x1{i}", [128, D], F32) for i in range(4)]
    ln3bf = [A(f"ln3bf{i}", [128, D], BF16) for i in range(2)]
    ln3T = [A(f"ln3T{i}", [128, KC, 128], BF16, is_transpose_dest=True)
            for i in range(4)]
    biasT = A("biasT", [1, D], BF16)
    tbl = {}
    for nm in CONST_NAMES:
        shp = _SHAPES.get(nm, [128, 512])
        dt = F32 if nm == "c_dl" else BF16
        tbl[nm] = A("t" + nm[1:], shp, dt)
    st1 = [A(f"st1_{i}", [128, 2, 4, 3], F32) for i in range(2)]
    ag1 = [A(f"ag1_{i}", [128, 20], F32) for i in range(2)]
    st2 = A("st2", [128, 2, 4, 3], F32)
    ag2 = A("ag2", [128, 20], F32)
    st3 = st1[0]
    ag3 = A("ag3", [128, 20], F32)
    st4 = A("st4", [128, 2, 4, 3], F32)
    ag4 = A("ag4", [128, 20], F32)
    rden = A("rden", [128, 16], F32)
    rs1 = A("rs1", [128, 512], BF16)
    rs2 = A("rs2", [128, 512], BF16)
    cbias = A("cbias", [128, 2], F32)

    def chunks4(ap3d):
        return ap3d.rearrange("p c n -> p (c n)").rearrange("p (c n) -> p c n", c=4)
    wr_al = [chunks4(qro[0][:]), chunks4(qro[1][:])]
    wi_al = [chunks4(khi[0][:]), chunks4(khi[1][:])]
    dsub_al = klo[0][:].rearrange("p c n -> p (c n)")[:, 0:24 * 128].rearrange(
        "p (c n) -> p c n", c=24)
    abuf0f = abuf[0][:].rearrange("p c n -> p (c n)")
    abuf1f = abuf[1][:].rearrange("p c n -> p (c n)")
    crow_dst = [ao[0][:], ao[1][:], abuf0f[:, D:2 * D], abuf1f[:, D:2 * D],
                vprev[0][:][:, 0:D], vprev[1][:][:, 0:D],
                x1[0][:].bitcast(BF16)[:, 0:D]]
    crow_lo = [c[:, 0:512] for c in crow_dst] + [rs1[:]]
    crow_hi = [c[:, 512:1024] for c in crow_dst] + [rs2[:]]
    srow_al = m_sb[:]
    t12a = abuf[0][:].rearrange("p c n -> p (c n)")[:, 0:D]
    t12b = abuf[1][:].rearrange("p c n -> p (c n)")[:, 0:D]
    t1T_al = aoT[0]
    t2T_al = aoT[1]
    mslice = xr[0][:]

    SB_REM = nc.sbuf_bytes_remaining
    ps = [nc.alloc_psum_tensor(f"ps{i}", [128, 512], F32) for i in range(8)]

    semnames = [
        "s_x0", "s_x1", "s_wqk0", "s_wqk1", "s_wqk2", "s_wqk3", "s_cst", "s_lt0", "s_lt1",
        "s_at0", "s_at1", "s_at2", "s_at3", "s_3t0", "s_3t1", "s_3t2", "s_3t3",
        "s_xr0", "s_xr1", "s_out0", "s_out1",
        "s_mbo", "s_cc", "s_mld", "s_mbi", "s_mb2", "s_mbt", "s_dbg", "s_wo", "s_bias", "s_wv",
        "c_qk", "c_v", "c_sim", "c_avr", "c_wo", "c_ff", "c_mbT", "c_mbM",
        "c_init", "c_ln1s", "c_nmr", "c_rope", "c_klo", "c_dv", "c_vprev",
        "c_l2g", "c_n2", "c_x1", "c_l3g", "c_n3", "c_l4g", "c_n4", "c_oadd",
        "c_r1", "c_l1a", "c_qke", "c_ve", "c_exp",
        "c_r2", "c_l2a", "c_r3", "c_l3a", "c_r4", "c_l4a", "c_mbe",
    ]
    sems = {}
    _ctxs = []
    for n in semnames:
        cm = nc.semaphore(n)
        sems[n] = cm.__enter__()
        _ctxs.append(cm)

    blkmgr = nc.Block()
    block = blkmgr.__enter__()

    def W(e, sem, val):
        if val > 0:
            e.wait_ge(sems[sem], val)

    def DR(e):
        if KDRAIN:
            e.drain()

    def DRI(e, sem, n=1):
        if KDRAIN:
            e.drain().then_inc(sems[sem], n)
        else:
            e.engine_nop().then_inc(sems[sem], n)

    def ln_aggr(v, stbuf, nt, wk):
        m_c = wk[:, 0:nt]
        var_c = wk[:, nt:2 * nt]
        negm_c = wk[:, 2 * nt:3 * nt]
        tmp_c = wk[:, 3 * nt:4 * nt]
        tmp2_c = wk[:, 4 * nt:5 * nt]
        sq_c = wk[:, 5 * nt:9 * nt].rearrange("p (t g) -> p t g", g=4)
        means = stbuf[:, :, :, 1]
        m2s = stbuf[:, :, :, 2]
        DR(v)
        v.reduce_sum(tmp_c.unsqueeze(2), means, axis=AX.X)
        v.tensor_mul(sq_c, means, means)
        v.reduce_sum(var_c.unsqueeze(2), m2s, axis=AX.X)
        DR(v)
        v.tensor_scalar_mul(m_c, tmp_c, 0.25)
        v.tensor_scalar_mul(negm_c, tmp_c, -0.25)
        v.reduce_sum(tmp2_c.unsqueeze(2), sq_c, axis=AX.X)
        v.tensor_scalar_mul(var_c, var_c, 1.0 / 1024.0)
        DR(v)
        v.tensor_mul(tmp_c, m_c, m_c)
        v.scalar_tensor_tensor(var_c, tmp2_c, 0.25, var_c, op0=ALU.mult,
                               op1=ALU.add)
        DR(v)
        v.tensor_sub(var_c, var_c, tmp_c)
        DR(v)
        v.tensor_scalar(var_c, var_c, EPS, None, op0=ALU.add)
        if KDRAIN:
            return v.drain()
        return v.engine_nop()

    @block.gpsimd
    def _(g):
        def load_x(t):
            if not (0 <= t < NT) or t // 4 > KLIM:
                return
            if t >= 2:
                W(g, f"s_lt{t % 2}", 16 * (t // 2))
            g.dma_start(out=x_sb[t % X_RING][:],
                        in_=xin[128 * t:128 * (t + 1), :]
                        ).then_inc(sems[f"s_x{t % 2}"], 16)

        for nm in CONST_NAMES:
            g.dma_start(out=tbl[nm][:], in_=c_in[nm]).then_inc(sems["s_cst"], 16)
        wfr4 = wfr.rearrange("(c p) n -> p c n", p=128)
        wfi4 = wfi.rearrange("(c p) n -> p c n", p=128)
        g.dma_start(out=wr_al[0], in_=wfr4[:, 0:4, :]).then_inc(sems["s_mbi"], 16)
        g.dma_start(out=wr_al[1], in_=wfr4[:, 4:8, :]).then_inc(sems["s_mbi"], 16)
        g.dma_start(out=wi_al[0], in_=wfi4[:, 0:4, :]).then_inc(sems["s_mbi"], 16)
        g.dma_start(out=wi_al[1], in_=wfi4[:, 4:8, :]).then_inc(sems["s_mbi"], 16)
        g.dma_start(out=dsub_al,
                    in_=dsub.rearrange("s (c p) n -> p (s c) n", p=128)
                    ).then_inc(sems["s_mbi"], 16)
        dc4 = c_dftC.rearrange("(c p) n -> p c n", p=128)
        ds4 = c_dftS.rearrange("(c p) n -> p c n", p=128)
        for e in range(7):
            g.dma_start(out=crow_dst[e],
                        in_=dc4[:, e:e + 1, :].rearrange("p c n -> p (c n)")
                        ).then_inc(sems["s_mb2"], 16)
        g.dma_start(out=rs1[:], in_=dc4[:, 7:8, 0:512].rearrange(
            "p c n -> p (c n)")).then_inc(sems["s_mb2"], 16)
        g.dma_start(out=rs2[:], in_=dc4[:, 7:8, 512:1024].rearrange(
            "p c n -> p (c n)")).then_inc(sems["s_mb2"], 16)
        g.dma_start(out=srow_al, in_=ds4).then_inc(sems["s_mb2"], 16)
        for t in range(4):
            load_x(t)
        g.dma_start(out=wv_sb[:], in_=wv.rearrange("(c p) n -> p c n", p=128)
                    ).then_inc(sems["s_wv"], 16)
        if KDBG:
            W(g, "s_lt0", 32)
            W(g, "s_lt1", 32)
            g.dma_start(out=dbg["d_ln1t"], in_=ln1T[0][:]).then_inc(sems["s_dbg"], 16)
        for t in range(4, 8):
            load_x(t)
        g.dma_start(out=wo_sb[:], in_=wo.rearrange("(c p) n -> p c n", p=128)
                    ).then_inc(sems["s_wo"], 16)
        g.dma_start(out=biasT[:], in_=fbias).then_inc(sems["s_bias"], 16)
        W(g, "c_mbe", 3)
        g.dma_start(out=ccin.ap(), in_=mslice).then_inc(sems["s_mbo"], 16)
        W(g, "s_mbo", 16)
        g.collective_compute(
            "AllGather", ALU.bypass, replica_groups=[list(range(NCORE))],
            ins=[ccin.ap().opt()], outs=[ccout.ap().opt()],
        ).then_inc(sems["s_cc"])
        W(g, "s_cc", 1)
        g.dma_start(out=m_sb[:], in_=ccout.ap().rearrange("(c p) n -> p c n", p=128)
                    ).then_inc(sems["s_mld"], 16)

        for sg in range(KLIM):
            if KDBG and sg == 1:
                W(g, "c_rope", 8)
                g.dma_start(out=dbg["d_qro"], in_=qro[0][:]).then_inc(sems["s_dbg"], 16)
                g.dma_start(out=dbg["d_khi"], in_=khi[0][:]).then_inc(sems["s_dbg"], 16)
                W(g, "c_klo", 1)
                g.dma_start(out=dbg["d_klo"], in_=klo[0][:, :, 128:512]).then_inc(sems["s_dbg"], 16)
                W(g, "c_exp", 4)
                g.dma_start(out=dbg["d_abuf"], in_=abuf[0][:].rearrange("p s (t n) -> p s t n", n=128)[:, :, 1:4:2, :]).then_inc(sems["s_dbg"], 16)
                W(g, "c_dv", 1)
                g.dma_start(out=dbg["d_ao"], in_=ao[0][:]).then_inc(sems["s_dbg"], 16)
            def load_xr(t):
                W(g, "c_x1", t - 1)
                if t == 0:
                    W(g, "s_mbo", 16)
                g.dma_start(out=xr[t % XR_RING][:],
                            in_=xin[128 * t:128 * (t + 1), :]
                            ).then_inc(sems[f"s_xr{t % 2}"], 16)

            b2 = sg - 2
            if 0 <= b2 < NS and b2 <= KLIM - 3:
                load_xr(4 * b2 + 2)
                load_xr(4 * b2 + 3)
            bb = sg - 1
            if 0 <= bb < NS and bb <= KLIM - 3:
                load_xr(4 * bb)
                load_xr(4 * bb + 1)
            for tt in range(SUP):
                load_x(4 * (sg + 2) + tt)

    @block.sync
    def _(sy):
        WQK_COVER.clear()
        WQK_CNT.clear()
        WQK_CNT.update({0: 0, 1: 0, 2: 0, 3: 0})
        W(sy, "c_mbe", 1)
        sy.dma_start_transpose(t1T_al[:], t12a.rearrange("p (c n) -> p c n", c=KC)
                               ).then_inc(sems["s_mbt"], 16)
        W(sy, "c_mbe", 2)
        sy.dma_start_transpose(t2T_al[:], t12b.rearrange("p (c n) -> p c n", c=KC)
                               ).then_inc(sems["s_mbt"], 16)
        for sg in range(-1, KLIM):
            if 0 <= sg < NS:
                for grp in range(4, 16):
                    G = 16 * sg + grp
                    W(sy, "c_qk", G - 3)
                    sy.dma_start(
                        out=wqk_sb[G % 4][:],
                        in_=wqk[8 * grp:8 * (grp + 1), :, :].rearrange(
                            "c p n -> p c n"),
                    ).then_inc(sems[f"s_wqk{G % 4}"], 16)
                    WQK_COVER[G] = (G % 4, WQK_CNT[G % 4] + 1)
                    WQK_CNT[G % 4] += 1
            s1 = sg + 1
            if 0 <= s1 < NS:
                if KDBG and s1 == 2:
                    W(sy, "s_dbg", 16)
                for tt in range(SUP):
                    t = 4 * s1 + tt
                    W(sy, "c_l1a", t + 1)
                    sy.dma_start_transpose(
                        ln1T[s1 % 2][:, :, 128 * tt:128 * (tt + 1)],
                        x_sb[t % X_RING][:].rearrange("p (c n) -> p c n", c=KC),
                    ).then_inc(sems[f"s_lt{t % 2}"], 16)
            s1w = sg + 1
            if 0 <= s1w < NS:
                for grp in range(4):
                    G = 16 * s1w + grp
                    if G >= 4:
                        W(sy, "c_qk", G - 3)
                    sy.dma_start(
                        out=wqk_sb[G % 4][:],
                        in_=wqk[8 * grp:8 * (grp + 1), :, :].rearrange(
                            "c p n -> p c n"),
                    ).then_inc(sems[f"s_wqk{G % 4}"], 16)
                    WQK_COVER[G] = (G % 4, WQK_CNT[G % 4] + 1)
                    WQK_CNT[G % 4] += 1
            bs = sg - 3
            if 0 <= bs < NS:
                for t in range(4 * bs, 4 * bs + 4):
                    W(sy, "c_oadd", t + 1)
                    sy.dma_start(out=yout[128 * t:128 * (t + 1), :],
                                 in_=x1[t % 4][:]).then_inc(sems[f"s_out{t % 2}"], 16)
            def ln3t_pairA():
                bo = sg - 2
                if not (0 <= bo < NS):
                    return
                for t in (4 * bo, 4 * bo + 1):
                    W(sy, "c_l3a", t + 1)
                    W(sy, "c_ff", t - 3)
                    sy.dma_start_transpose(
                        ln3T[t % 4][:],
                        ln3bf[t % 2][:].rearrange("p (c n) -> p c n", c=KC),
                    ).then_inc(sems[f"s_3t{t % 4}"], 16)

            bb = sg - 1
            if not (0 <= bb < NS):
                ln3t_pairA()
            if 0 <= bb < NS:
                for w in range(4 * bb, 4 * bb + 4):
                    if w == 4 * bb + 2:
                        ln3t_pairA()
                    W(sy, "c_dv", w + 1)
                    W(sy, "c_wo", w - 3)
                    if "aot" in KNO:
                        sy.dma_start(out=ccin.ap().bitcast(BF16)[0:128, 0:512],
                                     in_=ao[w % AO_RING][:, 0:512]
                                     ).then_inc(sems[f"s_at{w % 4}"], 16)
                        continue
                    sy.dma_start_transpose(
                        aoT[w % AOT_RING][:],
                        ao[w % AO_RING][:].rearrange("p (c n) -> p c n", c=KC),
                    ).then_inc(sems[f"s_at{w % 4}"], 16)
            bo = sg - 2
            if 0 <= bo < NS:
                for t in (4 * bo + 2, 4 * bo + 3):
                    W(sy, "c_l3a", t + 1)
                    W(sy, "c_ff", t - 3)
                    sy.dma_start_transpose(
                        ln3T[t % 4][:],
                        ln3bf[t % 2][:].rearrange("p (c n) -> p c n", c=KC),
                    ).then_inc(sems[f"s_3t{t % 4}"], 16)
        W(sy, "s_out0", 16 * NSTORE // 2)
        W(sy, "s_out1", 16 * NSTORE // 2)

    @block.tensor
    def _(te):
        W(te, "s_mbi", 16 * 5)
        for (pslo, pshi, rhs_a, rhs_b, suba, subb, csem) in (
                (ps[0], ps[1], wr_al, wi_al, 0, 8, 1),
                (ps[2], ps[3], wi_al, wr_al, 0, 16, 2)):
            last = None
            for e in range(8):
                lhs1 = dsub_al[:, suba + e, :]
                lhs2 = dsub_al[:, subb + e, :]
                r1 = rhs_a[e // 4][:, e % 4, :]
                r2 = rhs_b[e // 4][:, e % 4, :]
                st = (e == 0)
                te.matmul(pslo[:], lhs1, r1[:, 0:512], start=st, stop=False)
                te.matmul(pshi[:], lhs1, r1[:, 512:1024], start=st, stop=False)
                te.matmul(pslo[:], lhs2, r2[:, 0:512], start=False, stop=(e == 7))
                last = te.matmul(pshi[:], lhs2, r2[:, 512:1024],
                                 start=False, stop=(e == 7))
            last.then_inc(sems["c_mbT"], 1)
        W(te, "s_mbt", 32)
        W(te, "s_mb2", 16 * 10)
        last = None
        for e in range(8):
            sr = srow_al[:, e, :]
            st = (e == 0)
            te.matmul(ps[4][:], t1T_al[:, e, :], crow_lo[e], start=st, stop=False)
            te.matmul(ps[5][:], t1T_al[:, e, :], crow_hi[e], start=st, stop=False)
            te.matmul(ps[4][:], t2T_al[:, e, :], sr[:, 0:512],
                      start=False, stop=(e == 7))
            last = te.matmul(ps[5][:], t2T_al[:, e, :], sr[:, 512:1024],
                             start=False, stop=(e == 7))
        last.then_inc(sems["c_mbM"], 1)

        W(te, "s_cst", 16 * 6)
        W(te, "s_wv", 16)
        for sg in range(KLIM):
            if sg < NS:
                W(te, "s_lt0", 16 * (2 * sg + 2))
                W(te, "s_lt1", 16 * (2 * sg + 2))
                W(te, "c_l2a", 4 * (sg - 2))
                W(te, "c_oadd", 4 * (sg - 3))
                W(te, "c_exp", 32 * (sg - 1))
                W(te, "c_dv", 4 * (sg - 1))
                for i in range(16):
                    G = 16 * sg + i
                    cslot, ccnt = WQK_COVER[G]
                    W(te, f"s_wqk{cslot}", 16 * ccnt)
                    gi16 = G % 16
                    prevG = G - 6 if gi16 >= 6 else (G - 4 if gi16 <= 3 else G - 10)
                    if prevG >= 0:
                        W(te, "c_rope", prevG // 2 + 1)
                    bank = i % 6
                    last = None
                    for kk in range(8):
                        last = te.matmul(
                            ps[bank][:],
                            wqk_sb[G % 4][:, kk, :],
                            ln1T[sg % 2][:, kk, :],
                            start=(kk == 0), stop=(kk == 7))
                    last.then_inc(sems["c_qk"], 1)
                for tt in range(SUP):
                    W(te, "c_ve", 4 * sg + tt)
                    last = None
                    for kk in range(8):
                        lh = ln1T[sg % 2][:, kk, 128 * tt:128 * (tt + 1)]
                        te.matmul(ps[6][:], lh, wv_sb[:, kk, 0:512],
                                  start=(kk == 0), stop=(kk == 7))
                        last = te.matmul(ps[7][:], lh, wv_sb[:, kk, 512:1024],
                                         start=(kk == 0), stop=(kk == 7))
                    last.then_inc(sems["c_v"], 1)
            bf = sg - 3
            bo = sg - 2

            def f_tile(t):
                if t % 2 == 1:
                    W(te, "c_ve", 4 * min(sg + 1, NS))
                W(te, f"s_3t{t % 4}", 16 * (t // 4 + 1))
                W(te, "c_oadd", t - 1)
                b0, b1 = (4, 5) if t % 2 == 0 else (6, 7)
                for kk in range(8):
                    te.matmul(ps[b0][:], ln3T[t % 4][:, kk, :],
                              m_sb[:, kk, 0:512],
                              start=(kk == 0), stop=False)
                    te.matmul(ps[b1][:], ln3T[t % 4][:, kk, :],
                              m_sb[:, kk, 512:1024],
                              start=(kk == 0), stop=False)
                te.matmul(ps[b0][:], tbl["c_ones"][:], biasT[:, 0:512],
                          start=False, stop=True)
                te.matmul(ps[b1][:], tbl["c_ones"][:], biasT[:, 512:1024],
                          start=False, stop=True).then_inc(sems["c_ff"], 1)

            def w_tile(t):
                W(te, f"s_at{t % 4}", 16 * (t // 4 + 1))
                W(te, "c_l2a", t - 1)
                b0, b1 = (0, 1) if t % 2 == 0 else (2, 3)
                last = None
                for kk in range(8):
                    te.matmul(ps[b0][:], aoT[t % AOT_RING][:, kk, :],
                              wo_sb[:, kk, 0:512],
                              start=(kk == 0), stop=(kk == 7))
                    last = te.matmul(ps[b1][:], aoT[t % AOT_RING][:, kk, :],
                                     wo_sb[:, kk, 512:1024],
                                     start=(kk == 0), stop=(kk == 7))
                last.then_inc(sems["c_wo"], 1)

            if 0 <= bf < NS:
                W(te, "s_mld", 16)
                W(te, "s_bias", 16)
                W(te, "c_dv", 4 * min(sg - 1, NS))
                W(te, "c_rope", min(8 * sg + 6, 8 * NS))
                f_tile(4 * bf)
                f_tile(4 * bf + 1)
            if 0 <= bo < NS:
                W(te, "s_wo", 16)
                W(te, "c_exp", 32 * min(sg - 1, NS))
                W(te, "c_rope", 8 * min(sg + 1, NS))
                w_tile(4 * bo)
                w_tile(4 * bo + 1)
            if 0 <= bf < NS:
                f_tile(4 * bf + 2)
                f_tile(4 * bf + 3)
            if 0 <= bo < NS:
                w_tile(4 * bo + 2)
                w_tile(4 * bo + 3)
            b = sg - 1
            if 0 <= b < NS:
                W(te, "c_rope", 8 * min(sg + 1, NS))
                W(te, "c_ve", 4 * min(sg + 1, NS))
                W(te, "c_rope", 8 * sg)
                W(te, "c_klo", sg)
                W(te, "c_vprev", b)
                W(te, "c_l2a", 4 * (sg - 1))
                W(te, "c_oadd", 4 * (sg - 2))
                for w in range(4 * b, 4 * b + 4):
                    sl = w - 4 * b
                    for r in range(2):
                        gr = 2 * w + r
                        W(te, "c_exp", 8 * w + 4 * r)
                        first = [True] * 4
                        for gq in range(2):
                            for half in range(2):
                                if w == 0 and half == 0:
                                    continue
                                if "sim" in KNO:
                                    continue
                                slot = 2 * gq + half
                                chunk = 2 * r + gq
                                ksrc = klo[b % 2] if half == 0 else khi[b % 2]
                                for fs in range(2):
                                    cc = chunk + 4 * fs
                                    for jj in range(4):
                                        te.matmul(
                                            ps[jj][:, 128 * slot:128 * (slot + 1)],
                                            ksrc[32 * jj:32 * (jj + 1), cc,
                                                 128 * sl:128 * (sl + 1)],
                                            qro[b % 2][32 * jj:32 * (jj + 1), cc,
                                                       128 * sl:128 * (sl + 1)],
                                            start=(fs == 0 and first[jj]),
                                            stop=False,
                                            tile_position=(32 * jj, 0))
                                for jj in range(4):
                                    first[jj] = False
                        for jj in range(4):
                            if "mask" in KNO or "sim" in KNO:
                                te.sem_inc(sems["c_sim"], 1)
                                continue
                            if w == 0:
                                te.matmul(ps[jj][:, 128:256],
                                          tbl["c_id"][:], tbl["c_m12"][:, 128:256],
                                          start=False, stop=False)
                                te.matmul(ps[jj][:, 384:512],
                                          tbl["c_id"][:], tbl["c_m12"][:, 384:512],
                                          start=False, stop=True
                                          ).then_inc(sems["c_sim"], 1)
                            else:
                                te.matmul(ps[jj][:], tbl["c_id"][:],
                                          tbl["c_m12"][:],
                                          start=False, stop=True
                                          ).then_inc(sems["c_sim"], 1)
                        W(te, "c_dv", w)
                        if "av" in KNO:
                            W(te, "c_exp", 8 * w + 4 * r + 4)
                            te.sem_inc(sems["c_avr"], 1)
                            continue
                        last = None
                        for hp in (0, 4, 1, 5, 2, 6, 3, 7):
                            if hp < 4:
                                W(te, "c_exp", 8 * w + 4 * r + hp + 1)
                            head = 8 * r + hp
                            bank = 4 + 2 * r + hp // 4
                            col0 = 65 * (hp % 4)
                            halves = (1,) if w == 0 else (0, 1)
                            for half in halves:
                                T = w - 1 + half
                                if T // 4 == b:
                                    rhsv = vpr[b % 2][:, T % 4,
                                                      65 * head:65 * head + 65]
                                else:
                                    rhsv = vprev[(b - 1) % 2][:,
                                                              65 * head:65 * head + 65]
                                last = te.matmul(
                                    ps[bank][:, col0:col0 + 65],
                                    abuf[gr % 2][:, hp % 4,
                                                 128 * (2 * (hp // 4) + half):
                                                 128 * (2 * (hp // 4) + half) + 128],
                                    rhsv,
                                    start=(half == halves[0]), stop=(half == 1))
                        last.then_inc(sems["c_avr"], 1)
    @block.vector
    def _(v):
        ins = None
        for rr in range(2):
            ones_ap = vpr[rr][:].rearrange("p s (h c) -> p (s h) c", c=65)[:, :, 64:65]
            ins = v.memset(ones_ap, 1.0)
        v.memset(cbias[:, 0:1], EPS)
        v.memset(cbias[:, 1:2], EXPB)
        DRI(v, "c_init")
        W(v, "s_cst", 16 * 6)
        for sg in range(-1, KLIM):
            if 0 <= sg < NS:
                if KDBG and sg == 0:
                    W(v, "s_dbg", 16)
                if KDBG and sg == 2:
                    W(v, "s_dbg", 96)
                for pair in range(8):
                    gi = 16 * sg + 2 * pair
                    W(v, "c_qk", gi + 2)
                    isq = pair < 4
                    dst = qro[sg % 2] if isq else khi[sg % 2]
                    cos = tbl["c_cosq"]
                    sin = tbl["c_sinq"]
                    cch = pair % 4
                    fa = ps[(2 * pair) % 6][:]
                    sa = ps[(2 * pair + 1) % 6][:]
                    cosb = cos[:].unsqueeze(1).to_broadcast([128, 4, 128])
                    sinb = sin[:].unsqueeze(1).to_broadcast([128, 4, 128])

                    def t4(ap):
                        return ap.rearrange("p (a n) -> p a n", n=128)
                    v.tensor_tensor(t4(dst[:, cch, :]), t4(fa), cosb,
                                    op=ALU.mult)
                    v.tensor_tensor(t4(dst[:, cch + 4, :]), t4(sa), cosb,
                                    op=ALU.mult)
                    v.tensor_tensor(t4(rs1[:]), t4(sa), sinb, op=ALU.mult)
                    v.tensor_tensor(t4(rs2[:]), t4(fa), sinb, op=ALU.mult)
                    DR(v)
                    v.tensor_sub(dst[:, cch, :], dst[:, cch, :], rs1[:])
                    v.tensor_add(dst[:, cch + 4, :], dst[:, cch + 4, :], rs2[:])
                    DRI(v, "c_rope")
                cosd = tbl["c_dl"][:, 0:1]
                sind = tbl["c_dl"][:, 1:2]
                for phase in range(2):
                    for pair in range(4):
                        for sl2 in range(4):
                            tgl = 4 * sg - 1 + sl2
                            if tgl < 0 or tgl >= NT:
                                continue
                            src = khi[(tgl // 4) % 2]
                            ssl = tgl % 4
                            f_ = src[:, pair, 128 * ssl:128 * (ssl + 1)]
                            s_ = src[:, pair + 4, 128 * ssl:128 * (ssl + 1)]
                            of = klo[sg % 2][:, pair, 128 * sl2:128 * (sl2 + 1)]
                            os_ = klo[sg % 2][:, pair + 4,
                                              128 * sl2:128 * (sl2 + 1)]
                            if phase == 0:
                                v.tensor_scalar(of, s_, sind, None, op0=ALU.mult)
                                v.tensor_scalar(os_, f_, sind, None, op0=ALU.mult)
                            else:
                                v.scalar_tensor_tensor(of, f_, cosd, of,
                                                       op0=ALU.mult, op1=ALU.add)
                                v.scalar_tensor_tensor(os_, s_, cosd, os_,
                                                       op0=ALU.mult,
                                                       op1=ALU.subtract)
                    DR(v)
                DRI(v, "c_klo")
            s1 = sg + 1
            if 0 <= s1 < NS:
                for hf in range(2):
                    p1 = 2 * s1 + hf
                    stb = st1[p1 % 2]
                    for i2 in range(2):
                        t = 4 * s1 + 2 * hf + i2
                        W(v, f"s_x{t % 2}", 16 * (t // 2 + 1))
                        xx = x_sb[t % X_RING][:]
                        v.bn_stats(stb[:, i2, 0:2, :].rearrange("p a b -> p (a b)"),
                                   xx[:, 0:512])
                        v.bn_stats(stb[:, i2, 2:4, :].rearrange("p a b -> p (a b)"),
                                   xx[:, 512:1024])
                    ln_aggr(v, stb, 2, ag1[p1 % 2]).then_inc(sems["c_ln1s"], 1)
                    W(v, "c_r1", p1 + 1)
                    v.reciprocal(ag1[p1 % 2][:, 2:4], ag1[p1 % 2][:, 2:4])
                    DR(v)
                    v.tensor_mul(ag1[p1 % 2][:, 4:6], ag1[p1 % 2][:, 4:6], ag1[p1 % 2][:, 2:4])
                    DRI(v, "c_nmr")
            bf = sg - 3
            bo = sg - 2
            for hf in range(2):
                if 0 <= bf < NS:
                    pairt = [4 * bf + 2 * hf, 4 * bf + 2 * hf + 1]
                    for i2, t in enumerate(pairt):
                        W(v, "c_ff", t + 1)
                        pf = (ps[4], ps[5]) if t % 2 == 0 else (ps[6], ps[7])
                        v.bn_stats(st4[:, i2, 0:2, :].rearrange("p a b -> p (a b)"),
                                   pf[0][:])
                        v.bn_stats(st4[:, i2, 2:4, :].rearrange("p a b -> p (a b)"),
                                   pf[1][:])
                    ln_aggr(v, st4, 2, ag4).then_inc(sems["c_l4g"], 1)
                if 0 <= bo < NS:
                    pairw = [4 * bo + 2 * hf, 4 * bo + 2 * hf + 1]
                    for i2, t in enumerate(pairw):
                        W(v, "c_wo", t + 1)
                        pw = (ps[0], ps[1]) if t % 2 == 0 else (ps[2], ps[3])
                        v.bn_stats(st2[:, i2, 0:2, :].rearrange("p a b -> p (a b)"),
                                   pw[0][:])
                        v.bn_stats(st2[:, i2, 2:4, :].rearrange("p a b -> p (a b)"),
                                   pw[1][:])
                    ln_aggr(v, st2, 2, ag2).then_inc(sems["c_l2g"], 1)
                if 0 <= bo < NS:
                    W(v, "c_r2", 2 * bo + hf + 1)
                    v.reciprocal(ag2[:, 2:4], ag2[:, 2:4])
                    DR(v)
                    v.tensor_mul(ag2[:, 4:6], ag2[:, 4:6], ag2[:, 2:4])
                    DRI(v, "c_n2")
                if 0 <= bf < NS:
                    W(v, "c_r4", 2 * bf + hf + 1)
                    v.reciprocal(ag4[:, 2:4], ag4[:, 2:4])
                    DR(v)
                    v.tensor_mul(ag4[:, 4:6], ag4[:, 4:6], ag4[:, 2:4])
                    DR(v)
                    for i2, t in enumerate(pairt):
                        pf = (ps[4], ps[5]) if t % 2 == 0 else (ps[6], ps[7])
                        rstd = ag4[:, 2 + i2:3 + i2]
                        nmr = ag4[:, 4 + i2:5 + i2]
                        v.scalar_tensor_tensor(x1[t % 4][:, 0:512], pf[0][:],
                                               rstd, x1[t % 4][:, 0:512],
                                               op0=ALU.mult, op1=ALU.add)
                        v.scalar_tensor_tensor(x1[t % 4][:, 512:1024], pf[1][:],
                                               rstd, x1[t % 4][:, 512:1024],
                                               op0=ALU.mult, op1=ALU.add)
                        DR(v)
                        v.tensor_scalar(x1[t % 4][:, 0:512],
                                        x1[t % 4][:, 0:512], nmr, None,
                                        op0=ALU.add)
                        v.tensor_scalar(x1[t % 4][:, 512:1024],
                                        x1[t % 4][:, 512:1024], nmr, None,
                                        op0=ALU.add)
                        DRI(v, "c_oadd")
                if 0 <= bo < NS:
                    for i2, t in enumerate(pairw):
                        W(v, "c_l2a", t + 1)
                        W(v, f"s_xr{t % 2}", 16 * (t // 2 + 1))
                        v.tensor_add(x1[t % 4][:], x1[t % 4][:],
                                     xr[t % XR_RING][:])
                        DRI(v, "c_x1")
            bo3 = sg - 2

            def ln3_stats(hf):
                if not (0 <= bo3 < NS):
                    return
                pairt = [4 * bo3 + 2 * hf, 4 * bo3 + 2 * hf + 1]
                W(v, "c_l3a", 4 * bo3 + 2 * hf)
                for i2, t in enumerate(pairt):
                    v.bn_stats(st3[:, i2, 0:2, :].rearrange("p a b -> p (a b)"),
                               x1[t % 4][:, 0:512])
                    v.bn_stats(st3[:, i2, 2:4, :].rearrange("p a b -> p (a b)"),
                               x1[t % 4][:, 512:1024])
                ln_aggr(v, st3, 2, ag3).then_inc(sems["c_l3g"], 1)

            def ln3_nmr(hf):
                if not (0 <= bo3 < NS):
                    return
                W(v, "c_r3", 2 * bo3 + hf + 1)
                v.reciprocal(ag3[:, 2:4], ag3[:, 2:4])
                DR(v)
                v.tensor_mul(ag3[:, 4:6], ag3[:, 4:6], ag3[:, 2:4])
                DR(v)
                pairt = [4 * bo3 + 2 * hf, 4 * bo3 + 2 * hf + 1]
                for i2, t in enumerate(pairt):
                    if t >= 2:
                        W(v, f"s_3t{(t - 2) % 4}", 16 * ((t - 2) // 4 + 1))
                    v.tensor_scalar(ln3bf[t % 2][:], x1[t % 4][:],
                                    ag3[:, 2 + i2:3 + i2],
                                    ag3[:, 4 + i2:5 + i2],
                                    op0=ALU.mult, op1=ALU.add)
                    DRI(v, "c_l3a")

            bb = sg - 1
            if not (0 <= bb < NS):
                ln3_stats(0)
                ln3_nmr(0)
                ln3_stats(1)
                ln3_nmr(1)
            if 0 <= bb < NS:
                for w in range(4 * bb, 4 * bb + 4):
                    if KDBG and w == 2:
                        W(v, "s_dbg", 96)
                    W(v, "c_avr", 2 * w + 2)
                    if w >= 2:
                        W(v, f"s_at{(w - 2) % 4}", 16 * ((w - 2) // 4 + 1))
                    if "div" in KNO:
                        v.sem_inc(sems["c_dv"], 1)
                        continue
                    for ab in range(4):
                        pden = ps[4 + ab][:, 0:260].rearrange(
                            "p (h c) -> p h c", c=65)
                        v.reciprocal(
                            rden[:, 4 * ab:4 * ab + 4],
                            pden[:, :, 64:65].rearrange("p a b -> p (a b)"))
                    DR(v)
                    for ab in range(4):
                        pden = ps[4 + ab][:, 0:260].rearrange(
                            "p (h c) -> p h c", c=65)
                        h0 = 8 * (ab // 2) + 4 * (ab % 2)
                        v.tensor_tensor(
                            ao[w % AO_RING][:, 64 * h0:64 * h0 + 256].rearrange(
                                "p (h c) -> p h c", c=64),
                            pden[:, :, 0:64],
                            rden[:, 4 * ab:4 * ab + 4].unsqueeze(2).to_broadcast(
                                [128, 4, 64]),
                            op=ALU.mult)
                    DRI(v, "c_dv")
                    if w == 4 * bb + 1:
                        ln3_stats(0)
                    elif w == 4 * bb + 2:
                        ln3_nmr(0)
                    elif w == 4 * bb + 3:
                        ln3_stats(1)
            if 0 <= sg < NS:
                W(v, "c_ve", 4 * sg + 4)
                W(v, "c_avr", 8 * sg - 6)
                v.tensor_copy(vprev[sg % 2][:], vpr[sg % 2][:, 3, :])
                DRI(v, "c_vprev")
            if 0 <= bb < NS:
                ln3_nmr(1)

    @block.scalar
    def _(a):
        W(a, "c_mbT", 1)
        a.activation(out=t12a[:, 0:512], in_=ps[0][:], func=AF.Copy)
        a.activation(out=t12a[:, 512:1024], in_=ps[1][:], func=AF.Copy)
        DRI(a, "c_mbe")
        W(a, "c_mbT", 2)
        a.activation(out=t12b[:, 0:512], in_=ps[2][:], func=AF.Copy, scale=-1.0)
        a.activation(out=t12b[:, 512:1024], in_=ps[3][:], func=AF.Copy, scale=-1.0)
        DRI(a, "c_mbe")
        W(a, "c_mbM", 1)
        a.activation(out=mslice[:, 0:512], in_=ps[4][:], func=AF.Copy,
                     scale=1.0 / 1024.0)
        a.activation(out=mslice[:, 512:1024], in_=ps[5][:], func=AF.Copy,
                     scale=1.0 / 1024.0)
        DRI(a, "c_mbe")

        for sg in range(-1, KLIM):
            if 0 <= sg < NS:
                for tt in range(SUP):
                    W(a, "c_v", 4 * sg + tt + 1)
                    W(a, "c_init", 1)
                    vdst = vpr[sg % 2][:, tt, :].rearrange("p (h c) -> p h c", c=65)
                    a.activation(out=vdst[:, 0:8, 0:64], in_=ps[6][:].rearrange(
                        "p (h c) -> p h c", c=64), func=AF.Copy)
                    a.activation(out=vdst[:, 8:16, 0:64],
                                 in_=ps[7][:].rearrange(
                                     "p (h c) -> p h c", c=64), func=AF.Copy
                                 ).then_inc(sems["c_ve"], 1)
            s1 = sg + 1
            if 0 <= s1 < NS:
                for hf in range(2):
                    p1 = 2 * s1 + hf
                    ag = ag1[p1 % 2]
                    W(a, "c_ln1s", p1 + 1)
                    a.activation(out=ag[:, 2:4], in_=ag[:, 2:4],
                                 func=AF.Sqrt)
                    DRI(a, "c_r1")
                    W(a, "c_nmr", p1 + 1)
                    for i2 in range(2):
                        t = 4 * s1 + 2 * hf + i2
                        a.activation(out=x_sb[t % X_RING][:],
                                     in_=x_sb[t % X_RING][:],
                                     func=AF.Identity, scale=ag[:, 2 + i2:3 + i2],
                                     bias=ag[:, 4 + i2:5 + i2]
                                     ).then_inc(sems["c_l1a"], 1)
            bf = sg - 3
            bo = sg - 2
            for hf in range(2):
                if 0 <= bf < NS:
                    W(a, "c_l4g", 2 * bf + hf + 1)
                    a.activation(out=ag4[:, 2:4], in_=ag4[:, 2:4],
                                 func=AF.Sqrt)
                    DRI(a, "c_r4")
                if 0 <= bo < NS:
                    pairt = [4 * bo + 2 * hf, 4 * bo + 2 * hf + 1]
                    W(a, "c_l2g", 2 * bo + hf + 1)
                    a.activation(out=ag2[:, 2:4], in_=ag2[:, 2:4],
                                 func=AF.Sqrt)
                    DRI(a, "c_r2")
                    W(a, "c_n2", 2 * bo + hf + 1)
                    for i2, t in enumerate(pairt):
                        W(a, "c_oadd", t - 3)
                        W(a, f"s_out{t % 2}", 16 * (t // 2 - 1))
                        pw = (ps[0], ps[1]) if t % 2 == 0 else (ps[2], ps[3])
                        a.activation(out=x1[t % 4][:, 0:512], in_=pw[0][:],
                                     func=AF.Identity, scale=ag2[:, 2 + i2:3 + i2],
                                     bias=ag2[:, 4 + i2:5 + i2])
                        a.activation(out=x1[t % 4][:, 512:1024], in_=pw[1][:],
                                     func=AF.Identity, scale=ag2[:, 2 + i2:3 + i2],
                                     bias=ag2[:, 4 + i2:5 + i2]
                                     ).then_inc(sems["c_l2a"], 1)
            bo = sg - 2

            def act_sqrt3(hf):
                if not (0 <= bo < NS):
                    return
                W(a, "c_l3g", 2 * bo + hf + 1)
                a.activation(out=ag3[:, 2:4], in_=ag3[:, 2:4], func=AF.Sqrt)
                DRI(a, "c_r3")

            bb = sg - 1
            if 0 <= bb < NS:
                for w in range(4 * bb, 4 * bb + 4):
                    for r in range(2):
                        gr = 2 * w + r
                        if KDBG and gr == 2:
                            W(a, "s_dbg", 96)
                        for sb in range(4):
                            W(a, "c_sim", 8 * w + 4 * r + sb + 1)
                            W(a, "c_avr", gr - 1)
                            if "exp" in KNO:
                                a.sem_inc(sems["c_exp"], 1)
                                continue
                            if w == 0:
                                src = ps[sb][:].rearrange(
                                    "p (s n) -> p s n", n=128)[:, 1:4:2, :]
                                dst = abuf[gr % 2][:, sb, :].rearrange(
                                    "p (s n) -> p s n", n=128)[:, 1:4:2, :]
                            else:
                                src = ps[sb][:]
                                dst = abuf[gr % 2][:, sb, :]
                            a.activation(out=dst, in_=src, func=AF.Exp,
                                         scale=8.0, bias=cbias[:, 1:2]
                                         ).then_inc(sems["c_exp"], 1)
                    if w == 4 * bb + 1:
                        act_sqrt3(0)
            bo = sg - 2
            if 0 <= bo < NS and not (0 <= sg - 1 < NS):
                act_sqrt3(0)
            if 0 <= bo < NS:
                act_sqrt3(1)
    blkmgr.__exit__(None, None, None)
    for cm in reversed(_ctxs):
        cm.__exit__(None, None, None)
    return nc, SB_REM


def kernel(L0_attn_x, L0_attn_W_qkv, L0_attn_W_out,
           L0_ffn_w_freq_real, L0_ffn_w_freq_imag, L0_ffn_bias):
    x = np.asarray(L0_attn_x, dtype=np.float32)
    Wqkv = np.asarray(L0_attn_W_qkv, dtype=np.float32)
    Wout = np.ascontiguousarray(
        np.asarray(L0_attn_W_out, dtype=np.float32).astype(ml_dtypes.bfloat16))
    wfr = np.ascontiguousarray(
        np.asarray(L0_ffn_w_freq_real, dtype=np.float32).astype(ml_dtypes.bfloat16))
    wfi = np.ascontiguousarray(
        np.asarray(L0_ffn_w_freq_imag, dtype=np.float32).astype(ml_dtypes.bfloat16))
    fbias = np.ascontiguousarray(
        np.asarray(L0_ffn_bias, dtype=np.float32).reshape(1, D)
        .astype(ml_dtypes.bfloat16))

    perm = _quad_perm()
    subs = []
    for i in range(16):
        which = 0 if i < 8 else 1
        c = CHUNK_ORDER[i % 8]
        cols = which * D + perm[c * 128:(c + 1) * 128]
        for kk in range(8):
            subs.append(np.ascontiguousarray(Wqkv[128 * kk:128 * (kk + 1), cols]))
    wqk_host = np.stack(subs, axis=0).astype(ml_dtypes.bfloat16)

    wv_host = np.ascontiguousarray(Wqkv[:, 2 * D:3 * D]).astype(ml_dtypes.bfloat16)
    cst = _host_constants()

    nc, _ = build_nc()

    grid = (2.0 * np.pi / D) * ((np.arange(D)[:, None] * np.arange(D)[None, :]) % D)
    Cd = np.cos(grid)
    Sd = np.sin(grid)

    in_maps = []
    for r in range(NCORE):
        dsub = np.ascontiguousarray(np.stack([
            Cd[:, 128 * r:128 * (r + 1)],
            Sd[:, 128 * r:128 * (r + 1)],
            -Sd[:, 128 * r:128 * (r + 1)],
        ]).astype(ml_dtypes.bfloat16))
        im = {
            "xin": np.ascontiguousarray(x[r]),
            "wqk": wqk_host,
            "wv": wv_host,
            "wo": Wout,
            "wfr": wfr,
            "wfi": wfi,
            "fbias": fbias,
            "dsub": dsub,
            "c_dftC": cst["c_dftC"],
            "c_dftS": cst["c_dftS"],
        }
        for nm in CONST_NAMES:
            im[nm] = cst[nm]
        in_maps.append(im)

    res = run_bass_kernel_spmd(nc, in_maps, core_ids=list(range(NCORE)))
    out = np.stack([res.results[r]["yout"] for r in range(NCORE)], axis=0)
    if os.environ.get("KDBG", "0") == "1":
        kernel.dbg_results = res.results
    return out


# version 31
# speedup vs baseline: 1.0189x; 1.0189x over previous
import os
import numpy as np
import ml_dtypes
import concourse.bass as bass
import concourse.mybir as mybir
from concourse.bass_utils import run_bass_kernel_spmd

F32 = mybir.dt.float32
BF16 = mybir.dt.bfloat16
AF = mybir.ActivationFunctionType
ALU = mybir.AluOpType
AX = mybir.AxisListType

NCORE = 8
SEQ = 4096
D = 1024
H = 16
DH = 64
NT = 32
SUP = 4
NS = 8
KC = 8
EPS = 1e-5
EXPB = -8.0
MASKV = -30000.0

X_RING = 2
AO_RING = 2
AOT_RING = 4
XR_RING = 2

CHUNK_ORDER = [0, 4, 1, 5, 2, 6, 3, 7]
WQK_COVER = {}
WQK_CNT = {0: 0, 1: 0, 2: 0, 3: 0}
KLIM = int(os.environ.get("KLIM", str(NS + 3)))
KNO = set(os.environ.get("KNO", "").split(",")) - {""}
KDBG = os.environ.get("KDBG", "0") == "1"
KDRAIN = os.environ.get("KDRAIN", "1") == "1"
NSTORE = sum(1 for sg in range(KLIM) for t in range(4) if 0 <= sg - 3 < NS)


def _quad_perm():
    perm = np.zeros(1024, dtype=np.int64)
    for c in range(8):
        half = c // 4
        for p in range(128):
            head = 4 * (c % 4) + p // 32
            d = (p % 32) + 32 * half
            perm[c * 128 + p] = head * DH + d
    return perm


def _host_constants():
    cst = {}
    inv = 10000.0 ** (-(np.arange(32) / 32.0))
    t = np.arange(128)
    idx = np.arange(128) % 32
    phq = (128.0 + t)[None, :] * inv[idx][:, None]

    def tile4(a):
        return np.ascontiguousarray(np.tile(a, (1, SUP))).astype(ml_dtypes.bfloat16)

    cst["c_cosq"] = np.ascontiguousarray(np.cos(phq) * 0.125).astype(
        ml_dtypes.bfloat16)
    cst["c_sinq"] = np.ascontiguousarray(np.sin(phq) * 0.125).astype(
        ml_dtypes.bfloat16)
    j = np.arange(128)[:, None]
    q = np.arange(128)[None, :]
    m1 = np.where(j < q, MASKV, 0.0)
    m2 = np.where(j > q, MASKV, 0.0)
    cst["c_m12"] = np.ascontiguousarray(
        np.concatenate([m1, m2, m1, m2], axis=1)).astype(ml_dtypes.bfloat16)
    cst["c_id"] = np.eye(128, dtype=ml_dtypes.bfloat16)
    cst["c_ones"] = np.ones((1, 128), dtype=ml_dtypes.bfloat16)
    delta = 128.0 * inv[idx]
    cst["c_dl"] = np.stack([np.cos(delta), np.sin(delta)], axis=1).astype(np.float32)
    grid = (2.0 * np.pi / D) * ((np.arange(D)[:, None] * np.arange(D)[None, :]) % D)
    cst["c_dftC"] = np.cos(grid).astype(ml_dtypes.bfloat16)
    cst["c_dftS"] = np.sin(grid).astype(ml_dtypes.bfloat16)
    return cst


CONST_NAMES = ["c_cosq", "c_sinq", "c_m12", "c_id", "c_ones", "c_dl"]
_SHAPES = {"c_id": [128, 128], "c_ones": [1, 128], "c_dl": [128, 2],
           "c_cosq": [128, 128], "c_sinq": [128, 128]}


def build_nc():
    nc = bass.Bass("TRN2", target_bir_lowering=False)

    xin = nc.dram_tensor("xin", [SEQ, D], F32, kind="ExternalInput").ap()
    wqk = nc.dram_tensor("wqk", [128, 128, 128], BF16, kind="ExternalInput").ap()
    wv = nc.dram_tensor("wv", [D, D], BF16, kind="ExternalInput").ap()
    wo = nc.dram_tensor("wo", [D, D], BF16, kind="ExternalInput").ap()
    wfr = nc.dram_tensor("wfr", [D, D], BF16, kind="ExternalInput").ap()
    wfi = nc.dram_tensor("wfi", [D, D], BF16, kind="ExternalInput").ap()
    fbias = nc.dram_tensor("fbias", [1, D], BF16, kind="ExternalInput").ap()
    dsub = nc.dram_tensor("dsub", [3, D, 128], BF16, kind="ExternalInput").ap()
    c_in = {}
    for nm in CONST_NAMES:
        shp = _SHAPES.get(nm, [128, 512])
        dt = F32 if nm == "c_dl" else BF16
        c_in[nm] = nc.dram_tensor(nm, shp, dt, kind="ExternalInput").ap()
    c_dftC = nc.dram_tensor("c_dftC", [D, D], BF16, kind="ExternalInput").ap()
    c_dftS = nc.dram_tensor("c_dftS", [D, D], BF16, kind="ExternalInput").ap()
    yout = nc.dram_tensor("yout", [SEQ, D], F32, kind="ExternalOutput").ap()
    dbg = {}
    if KDBG:
        for nm, shp in (("d_ln1t", [128, KC, 512]), ("d_qkst", [128, 2, 512]),
                        ("d_qro", [128, KC, 512]), ("d_khi", [128, KC, 512]),
                        ("d_klo", [128, KC, 384]), ("d_abuf", [128, 4, 2, 128]),
                        ("d_ao", [128, D])):
            dbg[nm] = nc.dram_tensor(nm, shp, BF16, kind="ExternalOutput").ap()

    ccin = nc.dram_tensor("ccin", [128, D], F32, kind="Internal")
    ccout = nc.dram_tensor("ccout", [D, D], F32, kind="Internal")

    A = nc.alloc_sbuf_tensor
    wv_sb = A("wv_sb", [128, KC, D], BF16)
    wo_sb = A("wo_sb", [128, KC, D], BF16)
    m_sb = A("m_sb", [128, KC, D], BF16)
    wqk_flat = A("wqk_sb", [128, 32, 128], BF16)
    wqk_sb = [wqk_flat[:][:, 8 * i:8 * (i + 1), :] for i in range(4)]
    x_sb = [A(f"x_sb{i}", [128, D], BF16) for i in range(X_RING)]
    ln1T = [A(f"ln1T{i}", [128, KC, 512], BF16, is_transpose_dest=True)
            for i in range(2)]
    qro = [A(f"qro{i}", [128, KC, 512], BF16) for i in range(2)]
    khi = [A(f"khi{i}", [128, KC, 512], BF16) for i in range(2)]
    klo = [A(f"klo{i}", [128, KC, 512], BF16) for i in range(2)]
    vpr = [A(f"vpr{i}", [128, SUP, H * 65], BF16) for i in range(2)]
    vprev = [A(f"vprev{i}", [128, H * 65], BF16) for i in range(2)]
    abuf = [A(f"abuf{i}", [128, 4, 512], BF16) for i in range(2)]
    ao = [A(f"ao{i}", [128, D], BF16) for i in range(AO_RING)]
    aoT = [A(f"aoT{i}", [128, KC, 128], BF16, is_transpose_dest=True)
           for i in range(AOT_RING)]
    xr = [A(f"xr{i}", [128, D], F32) for i in range(XR_RING)]
    x1 = [A(f"x1{i}", [128, D], F32) for i in range(4)]
    ln3bf = [A(f"ln3bf{i}", [128, D], BF16) for i in range(2)]
    ln3T = [A(f"ln3T{i}", [128, KC, 128], BF16, is_transpose_dest=True)
            for i in range(4)]
    biasT = A("biasT", [1, D], BF16)
    tbl = {}
    for nm in CONST_NAMES:
        shp = _SHAPES.get(nm, [128, 512])
        dt = F32 if nm == "c_dl" else BF16
        tbl[nm] = A("t" + nm[1:], shp, dt)
    st1 = [A(f"st1_{i}", [128, 2, 4, 3], F32) for i in range(2)]
    ag1 = [A(f"ag1_{i}", [128, 20], F32) for i in range(2)]
    st2 = A("st2", [128, 2, 4, 3], F32)
    ag2 = A("ag2", [128, 20], F32)
    st3 = st1[0]
    ag3 = A("ag3", [128, 20], F32)
    st4 = A("st4", [128, 2, 4, 3], F32)
    ag4 = A("ag4", [128, 20], F32)
    rden = A("rden", [128, 16], F32)
    rs1 = A("rs1", [128, 512], BF16)
    rs2 = A("rs2", [128, 512], BF16)
    cbias = A("cbias", [128, 2], F32)

    def chunks4(ap3d):
        return ap3d.rearrange("p c n -> p (c n)").rearrange("p (c n) -> p c n", c=4)
    wr_al = [chunks4(qro[0][:]), chunks4(qro[1][:])]
    wi_al = [chunks4(khi[0][:]), chunks4(khi[1][:])]
    dsub_al = klo[0][:].rearrange("p c n -> p (c n)")[:, 0:24 * 128].rearrange(
        "p (c n) -> p c n", c=24)
    abuf0f = abuf[0][:].rearrange("p c n -> p (c n)")
    abuf1f = abuf[1][:].rearrange("p c n -> p (c n)")
    crow_dst = [ao[0][:], ao[1][:], abuf0f[:, D:2 * D], abuf1f[:, D:2 * D],
                vprev[0][:][:, 0:D], vprev[1][:][:, 0:D],
                x1[0][:].bitcast(BF16)[:, 0:D]]
    crow_lo = [c[:, 0:512] for c in crow_dst] + [rs1[:]]
    crow_hi = [c[:, 512:1024] for c in crow_dst] + [rs2[:]]
    srow_al = m_sb[:]
    t12a = abuf[0][:].rearrange("p c n -> p (c n)")[:, 0:D]
    t12b = abuf[1][:].rearrange("p c n -> p (c n)")[:, 0:D]
    t1T_al = aoT[0]
    t2T_al = aoT[1]
    mslice = xr[0][:]

    SB_REM = nc.sbuf_bytes_remaining
    ps = [nc.alloc_psum_tensor(f"ps{i}", [128, 512], F32) for i in range(8)]

    semnames = [
        "s_x0", "s_x1", "s_wqk0", "s_wqk1", "s_wqk2", "s_wqk3", "s_cst", "s_lt0", "s_lt1",
        "s_at0", "s_at1", "s_at2", "s_at3", "s_3t0", "s_3t1", "s_3t2", "s_3t3",
        "s_xr0", "s_xr1", "s_out0", "s_out1",
        "s_mbo", "s_cc", "s_mld", "s_mbi", "s_mb2", "s_mbt", "s_dbg", "s_wo", "s_bias", "s_wv",
        "c_qk", "c_v", "c_sim", "c_avr", "c_wo", "c_ff", "c_mbT", "c_mbM",
        "c_init", "c_ln1s", "c_nmr", "c_rope", "c_klo", "c_dv", "c_vprev",
        "c_l2g", "c_n2", "c_x1", "c_l3g", "c_n3", "c_l4g", "c_n4", "c_oadd",
        "c_r1", "c_l1a", "c_qke", "c_ve", "c_exp",
        "c_r2", "c_l2a", "c_r3", "c_l3a", "c_r4", "c_l4a", "c_mbe",
    ]
    sems = {}
    _ctxs = []
    for n in semnames:
        cm = nc.semaphore(n)
        sems[n] = cm.__enter__()
        _ctxs.append(cm)

    blkmgr = nc.Block()
    block = blkmgr.__enter__()

    def W(e, sem, val):
        if val > 0:
            e.wait_ge(sems[sem], val)

    def DR(e):
        if KDRAIN:
            e.drain()

    def DRI(e, sem, n=1):
        if KDRAIN:
            e.drain().then_inc(sems[sem], n)
        else:
            e.engine_nop().then_inc(sems[sem], n)

    def ln_aggr(v, stbuf, nt, wk):
        m_c = wk[:, 0:nt]
        var_c = wk[:, nt:2 * nt]
        negm_c = wk[:, 2 * nt:3 * nt]
        tmp_c = wk[:, 3 * nt:4 * nt]
        tmp2_c = wk[:, 4 * nt:5 * nt]
        sq_c = wk[:, 5 * nt:9 * nt].rearrange("p (t g) -> p t g", g=4)
        means = stbuf[:, :, :, 1]
        m2s = stbuf[:, :, :, 2]
        DR(v)
        v.reduce_sum(tmp_c.unsqueeze(2), means, axis=AX.X)
        v.tensor_mul(sq_c, means, means)
        v.reduce_sum(var_c.unsqueeze(2), m2s, axis=AX.X)
        DR(v)
        v.tensor_scalar_mul(m_c, tmp_c, 0.25)
        v.tensor_scalar_mul(negm_c, tmp_c, -0.25)
        v.reduce_sum(tmp2_c.unsqueeze(2), sq_c, axis=AX.X)
        v.tensor_scalar_mul(var_c, var_c, 1.0 / 1024.0)
        DR(v)
        v.tensor_mul(tmp_c, m_c, m_c)
        v.scalar_tensor_tensor(var_c, tmp2_c, 0.25, var_c, op0=ALU.mult,
                               op1=ALU.add)
        DR(v)
        v.tensor_sub(var_c, var_c, tmp_c)
        DR(v)
        v.tensor_scalar(var_c, var_c, EPS, None, op0=ALU.add)
        if KDRAIN:
            return v.drain()
        return v.engine_nop()

    @block.gpsimd
    def _(g):
        def load_x(t):
            if not (0 <= t < NT) or t // 4 > KLIM:
                return
            if t >= 2:
                W(g, f"s_lt{t % 2}", 16 * (t // 2))
            g.dma_start(out=x_sb[t % X_RING][:],
                        in_=xin[128 * t:128 * (t + 1), :]
                        ).then_inc(sems[f"s_x{t % 2}"], 16)

        for nm in CONST_NAMES:
            g.dma_start(out=tbl[nm][:], in_=c_in[nm]).then_inc(sems["s_cst"], 16)
        wfr4 = wfr.rearrange("(c p) n -> p c n", p=128)
        wfi4 = wfi.rearrange("(c p) n -> p c n", p=128)
        g.dma_start(out=wr_al[0], in_=wfr4[:, 0:4, :]).then_inc(sems["s_mbi"], 16)
        g.dma_start(out=wr_al[1], in_=wfr4[:, 4:8, :]).then_inc(sems["s_mbi"], 16)
        g.dma_start(out=wi_al[0], in_=wfi4[:, 0:4, :]).then_inc(sems["s_mbi"], 16)
        g.dma_start(out=wi_al[1], in_=wfi4[:, 4:8, :]).then_inc(sems["s_mbi"], 16)
        g.dma_start(out=dsub_al,
                    in_=dsub.rearrange("s (c p) n -> p (s c) n", p=128)
                    ).then_inc(sems["s_mbi"], 16)
        dc4 = c_dftC.rearrange("(c p) n -> p c n", p=128)
        ds4 = c_dftS.rearrange("(c p) n -> p c n", p=128)
        for e in range(7):
            g.dma_start(out=crow_dst[e],
                        in_=dc4[:, e:e + 1, :].rearrange("p c n -> p (c n)")
                        ).then_inc(sems["s_mb2"], 16)
        g.dma_start(out=rs1[:], in_=dc4[:, 7:8, 0:512].rearrange(
            "p c n -> p (c n)")).then_inc(sems["s_mb2"], 16)
        g.dma_start(out=rs2[:], in_=dc4[:, 7:8, 512:1024].rearrange(
            "p c n -> p (c n)")).then_inc(sems["s_mb2"], 16)
        g.dma_start(out=srow_al, in_=ds4).then_inc(sems["s_mb2"], 16)
        for t in range(4):
            load_x(t)
        g.dma_start(out=wv_sb[:], in_=wv.rearrange("(c p) n -> p c n", p=128)
                    ).then_inc(sems["s_wv"], 16)
        if KDBG:
            W(g, "s_lt0", 32)
            W(g, "s_lt1", 32)
            g.dma_start(out=dbg["d_ln1t"], in_=ln1T[0][:]).then_inc(sems["s_dbg"], 16)
        for t in range(4, 8):
            load_x(t)
        g.dma_start(out=wo_sb[:], in_=wo.rearrange("(c p) n -> p c n", p=128)
                    ).then_inc(sems["s_wo"], 16)
        g.dma_start(out=biasT[:], in_=fbias).then_inc(sems["s_bias"], 16)
        W(g, "c_mbe", 3)
        g.dma_start(out=ccin.ap(), in_=mslice).then_inc(sems["s_mbo"], 16)
        W(g, "s_mbo", 16)
        g.collective_compute(
            "AllGather", ALU.bypass, replica_groups=[list(range(NCORE))],
            ins=[ccin.ap().opt()], outs=[ccout.ap().opt()],
        ).then_inc(sems["s_cc"])
        W(g, "s_cc", 1)
        g.dma_start(out=m_sb[:], in_=ccout.ap().rearrange("(c p) n -> p c n", p=128)
                    ).then_inc(sems["s_mld"], 16)

        for sg in range(KLIM):
            if KDBG and sg == 1:
                W(g, "c_rope", 8)
                g.dma_start(out=dbg["d_qro"], in_=qro[0][:]).then_inc(sems["s_dbg"], 16)
                g.dma_start(out=dbg["d_khi"], in_=khi[0][:]).then_inc(sems["s_dbg"], 16)
                W(g, "c_klo", 1)
                g.dma_start(out=dbg["d_klo"], in_=klo[0][:, :, 128:512]).then_inc(sems["s_dbg"], 16)
                W(g, "c_exp", 4)
                g.dma_start(out=dbg["d_abuf"], in_=abuf[0][:].rearrange("p s (t n) -> p s t n", n=128)[:, :, 1:4:2, :]).then_inc(sems["s_dbg"], 16)
                W(g, "c_dv", 1)
                g.dma_start(out=dbg["d_ao"], in_=ao[0][:]).then_inc(sems["s_dbg"], 16)
            def load_xr(t):
                W(g, "c_x1", t - 1)
                if t == 0:
                    W(g, "s_mbo", 16)
                g.dma_start(out=xr[t % XR_RING][:],
                            in_=xin[128 * t:128 * (t + 1), :]
                            ).then_inc(sems[f"s_xr{t % 2}"], 16)

            b2 = sg - 2
            if 0 <= b2 < NS and b2 <= KLIM - 3:
                load_xr(4 * b2 + 2)
                load_xr(4 * b2 + 3)
            bb = sg - 1
            if 0 <= bb < NS and bb <= KLIM - 3:
                load_xr(4 * bb)
                load_xr(4 * bb + 1)
            for tt in range(SUP):
                load_x(4 * (sg + 2) + tt)

    @block.sync
    def _(sy):
        WQK_COVER.clear()
        WQK_CNT.clear()
        WQK_CNT.update({0: 0, 1: 0, 2: 0, 3: 0})
        W(sy, "c_mbe", 1)
        sy.dma_start_transpose(t1T_al[:], t12a.rearrange("p (c n) -> p c n", c=KC)
                               ).then_inc(sems["s_mbt"], 16)
        W(sy, "c_mbe", 2)
        sy.dma_start_transpose(t2T_al[:], t12b.rearrange("p (c n) -> p c n", c=KC)
                               ).then_inc(sems["s_mbt"], 16)
        for sg in range(-1, KLIM):
            if 0 <= sg < NS:
                for grp in range(4, 16):
                    G = 16 * sg + grp
                    W(sy, "c_qk", G - 3)
                    sy.dma_start(
                        out=wqk_sb[G % 4][:],
                        in_=wqk[8 * grp:8 * (grp + 1), :, :].rearrange(
                            "c p n -> p c n"),
                    ).then_inc(sems[f"s_wqk{G % 4}"], 16)
                    WQK_COVER[G] = (G % 4, WQK_CNT[G % 4] + 1)
                    WQK_CNT[G % 4] += 1
            s1 = sg + 1
            if 0 <= s1 < NS:
                if KDBG and s1 == 2:
                    W(sy, "s_dbg", 16)
                for tt in range(SUP):
                    t = 4 * s1 + tt
                    W(sy, "c_l1a", t + 1)
                    sy.dma_start_transpose(
                        ln1T[s1 % 2][:, :, 128 * tt:128 * (tt + 1)],
                        x_sb[t % X_RING][:].rearrange("p (c n) -> p c n", c=KC),
                    ).then_inc(sems[f"s_lt{t % 2}"], 16)
            s1w = sg + 1
            if 0 <= s1w < NS:
                for grp in range(4):
                    G = 16 * s1w + grp
                    if G >= 4:
                        W(sy, "c_qk", G - 3)
                    sy.dma_start(
                        out=wqk_sb[G % 4][:],
                        in_=wqk[8 * grp:8 * (grp + 1), :, :].rearrange(
                            "c p n -> p c n"),
                    ).then_inc(sems[f"s_wqk{G % 4}"], 16)
                    WQK_COVER[G] = (G % 4, WQK_CNT[G % 4] + 1)
                    WQK_CNT[G % 4] += 1
            bs = sg - 3
            if 0 <= bs < NS:
                for t in range(4 * bs, 4 * bs + 4):
                    W(sy, "c_oadd", t + 1)
                    sy.dma_start(out=yout[128 * t:128 * (t + 1), :],
                                 in_=x1[t % 4][:]).then_inc(sems[f"s_out{t % 2}"], 16)
            def ln3t_pairA():
                bo = sg - 2
                if not (0 <= bo < NS):
                    return
                for t in (4 * bo, 4 * bo + 1):
                    W(sy, "c_l3a", t + 1)
                    W(sy, "c_ff", t - 3)
                    sy.dma_start_transpose(
                        ln3T[t % 4][:],
                        ln3bf[t % 2][:].rearrange("p (c n) -> p c n", c=KC),
                    ).then_inc(sems[f"s_3t{t % 4}"], 16)

            bb = sg - 1
            if not (0 <= bb < NS):
                ln3t_pairA()
            if 0 <= bb < NS:
                for w in range(4 * bb, 4 * bb + 4):
                    if w == 4 * bb + 2:
                        ln3t_pairA()
                    W(sy, "c_dv", w + 1)
                    W(sy, "c_wo", w - 3)
                    if "aot" in KNO:
                        sy.dma_start(out=ccin.ap().bitcast(BF16)[0:128, 0:512],
                                     in_=ao[w % AO_RING][:, 0:512]
                                     ).then_inc(sems[f"s_at{w % 4}"], 16)
                        continue
                    sy.dma_start_transpose(
                        aoT[w % AOT_RING][:],
                        ao[w % AO_RING][:].rearrange("p (c n) -> p c n", c=KC),
                    ).then_inc(sems[f"s_at{w % 4}"], 16)
            bo = sg - 2
            if 0 <= bo < NS:
                for t in (4 * bo + 2, 4 * bo + 3):
                    W(sy, "c_l3a", t + 1)
                    W(sy, "c_ff", t - 3)
                    sy.dma_start_transpose(
                        ln3T[t % 4][:],
                        ln3bf[t % 2][:].rearrange("p (c n) -> p c n", c=KC),
                    ).then_inc(sems[f"s_3t{t % 4}"], 16)
        W(sy, "s_out0", 16 * NSTORE // 2)
        W(sy, "s_out1", 16 * NSTORE // 2)

    @block.tensor
    def _(te):
        W(te, "s_mbi", 16 * 5)
        for (pslo, pshi, rhs_a, rhs_b, suba, subb, csem) in (
                (ps[0], ps[1], wr_al, wi_al, 0, 8, 1),
                (ps[2], ps[3], wi_al, wr_al, 0, 16, 2)):
            last = None
            for e in range(8):
                lhs1 = dsub_al[:, suba + e, :]
                lhs2 = dsub_al[:, subb + e, :]
                r1 = rhs_a[e // 4][:, e % 4, :]
                r2 = rhs_b[e // 4][:, e % 4, :]
                st = (e == 0)
                te.matmul(pslo[:], lhs1, r1[:, 0:512], start=st, stop=False)
                te.matmul(pshi[:], lhs1, r1[:, 512:1024], start=st, stop=False)
                te.matmul(pslo[:], lhs2, r2[:, 0:512], start=False, stop=(e == 7))
                last = te.matmul(pshi[:], lhs2, r2[:, 512:1024],
                                 start=False, stop=(e == 7))
            last.then_inc(sems["c_mbT"], 1)
        W(te, "s_mbt", 32)
        W(te, "s_mb2", 16 * 10)
        last = None
        for e in range(8):
            sr = srow_al[:, e, :]
            st = (e == 0)
            te.matmul(ps[4][:], t1T_al[:, e, :], crow_lo[e], start=st, stop=False)
            te.matmul(ps[5][:], t1T_al[:, e, :], crow_hi[e], start=st, stop=False)
            te.matmul(ps[4][:], t2T_al[:, e, :], sr[:, 0:512],
                      start=False, stop=(e == 7))
            last = te.matmul(ps[5][:], t2T_al[:, e, :], sr[:, 512:1024],
                             start=False, stop=(e == 7))
        last.then_inc(sems["c_mbM"], 1)

        W(te, "s_cst", 16 * 6)
        W(te, "s_wv", 16)
        for sg in range(KLIM):
            if sg < NS:
                W(te, "s_lt0", 16 * (2 * sg + 2))
                W(te, "s_lt1", 16 * (2 * sg + 2))
                W(te, "c_l2a", 4 * (sg - 2))
                W(te, "c_oadd", 4 * (sg - 3))
                W(te, "c_exp", 32 * (sg - 1))
                W(te, "c_dv", 4 * (sg - 1))
                for i in range(16):
                    G = 16 * sg + i
                    cslot, ccnt = WQK_COVER[G]
                    W(te, f"s_wqk{cslot}", 16 * ccnt)
                    gi16 = G % 16
                    prevG = G - 6 if gi16 >= 6 else (G - 4 if gi16 <= 3 else G - 10)
                    if prevG >= 0:
                        W(te, "c_rope", prevG // 2 + 1)
                    bank = i % 6
                    last = None
                    for kk in range(8):
                        last = te.matmul(
                            ps[bank][:],
                            wqk_sb[G % 4][:, kk, :],
                            ln1T[sg % 2][:, kk, :],
                            start=(kk == 0), stop=(kk == 7))
                    last.then_inc(sems["c_qk"], 1)
                for tt in range(SUP):
                    W(te, "c_ve", 4 * sg + tt)
                    last = None
                    for kk in range(8):
                        lh = ln1T[sg % 2][:, kk, 128 * tt:128 * (tt + 1)]
                        te.matmul(ps[6][:], lh, wv_sb[:, kk, 0:512],
                                  start=(kk == 0), stop=(kk == 7))
                        last = te.matmul(ps[7][:], lh, wv_sb[:, kk, 512:1024],
                                         start=(kk == 0), stop=(kk == 7))
                    last.then_inc(sems["c_v"], 1)
            bf = sg - 3
            bo = sg - 2

            def f_tile(t):
                if t % 2 == 1:
                    W(te, "c_ve", 4 * min(sg + 1, NS))
                W(te, f"s_3t{t % 4}", 16 * (t // 4 + 1))
                W(te, "c_oadd", t - 1)
                b0, b1 = (4, 5) if t % 2 == 0 else (6, 7)
                for kk in range(8):
                    te.matmul(ps[b0][:], ln3T[t % 4][:, kk, :],
                              m_sb[:, kk, 0:512],
                              start=(kk == 0), stop=False)
                    te.matmul(ps[b1][:], ln3T[t % 4][:, kk, :],
                              m_sb[:, kk, 512:1024],
                              start=(kk == 0), stop=False)
                te.matmul(ps[b0][:], tbl["c_ones"][:], biasT[:, 0:512],
                          start=False, stop=True)
                te.matmul(ps[b1][:], tbl["c_ones"][:], biasT[:, 512:1024],
                          start=False, stop=True).then_inc(sems["c_ff"], 1)

            def w_tile(t):
                W(te, f"s_at{t % 4}", 16 * (t // 4 + 1))
                W(te, "c_l2a", t - 1)
                b0, b1 = (0, 1) if t % 2 == 0 else (2, 3)
                last = None
                for kk in range(8):
                    te.matmul(ps[b0][:], aoT[t % AOT_RING][:, kk, :],
                              wo_sb[:, kk, 0:512],
                              start=(kk == 0), stop=(kk == 7))
                    last = te.matmul(ps[b1][:], aoT[t % AOT_RING][:, kk, :],
                                     wo_sb[:, kk, 512:1024],
                                     start=(kk == 0), stop=(kk == 7))
                last.then_inc(sems["c_wo"], 1)

            if 0 <= bf < NS:
                W(te, "s_mld", 16)
                W(te, "s_bias", 16)
                W(te, "c_dv", 4 * min(sg - 1, NS))
                W(te, "c_rope", min(8 * sg + 6, 8 * NS))
                f_tile(4 * bf)
                f_tile(4 * bf + 1)
            if 0 <= bo < NS:
                W(te, "s_wo", 16)
                W(te, "c_exp", 32 * min(sg - 1, NS))
                W(te, "c_rope", 8 * min(sg + 1, NS))
                w_tile(4 * bo)
                w_tile(4 * bo + 1)
            if 0 <= bf < NS:
                f_tile(4 * bf + 2)
                f_tile(4 * bf + 3)
            if 0 <= bo < NS:
                w_tile(4 * bo + 2)
                w_tile(4 * bo + 3)
            b = sg - 1
            if 0 <= b < NS:
                W(te, "c_rope", 8 * min(sg + 1, NS))
                W(te, "c_ve", 4 * min(sg + 1, NS))
                W(te, "c_rope", 8 * sg)
                W(te, "c_klo", sg)
                W(te, "c_vprev", b)
                W(te, "c_l2a", 4 * (sg - 1))
                W(te, "c_oadd", 4 * (sg - 2))
                for w in range(4 * b, 4 * b + 4):
                    sl = w - 4 * b
                    for r in range(2):
                        gr = 2 * w + r
                        for jj in range(4):
                            W(te, "c_exp", 4 * gr + jj - 3)
                            if "sim" in KNO or "mask" in KNO:
                                if "sim" not in KNO:
                                    first = True
                                    for gq in range(2):
                                        for half in range(2):
                                            if w == 0 and half == 0:
                                                continue
                                            slot = 2 * gq + half
                                            chunk = 2 * r + gq
                                            ksrc = (klo[b % 2] if half == 0
                                                    else khi[b % 2])
                                            for fs in range(2):
                                                cc = chunk + 4 * fs
                                                te.matmul(
                                                    ps[jj][:, 128 * slot:
                                                           128 * (slot + 1)],
                                                    ksrc[32 * jj:32 * (jj + 1), cc,
                                                         128 * sl:128 * (sl + 1)],
                                                    qro[b % 2][32 * jj:
                                                               32 * (jj + 1), cc,
                                                               128 * sl:
                                                               128 * (sl + 1)],
                                                    start=(fs == 0 and first),
                                                    stop=False,
                                                    tile_position=(32 * jj, 0))
                                            first = False
                                te.sem_inc(sems["c_sim"], 1)
                                continue
                            first = True
                            for gq in range(2):
                                for half in range(2):
                                    if w == 0 and half == 0:
                                        continue
                                    slot = 2 * gq + half
                                    chunk = 2 * r + gq
                                    ksrc = klo[b % 2] if half == 0 else khi[b % 2]
                                    for fs in range(2):
                                        cc = chunk + 4 * fs
                                        te.matmul(
                                            ps[jj][:, 128 * slot:128 * (slot + 1)],
                                            ksrc[32 * jj:32 * (jj + 1), cc,
                                                 128 * sl:128 * (sl + 1)],
                                            qro[b % 2][32 * jj:32 * (jj + 1), cc,
                                                       128 * sl:128 * (sl + 1)],
                                            start=(fs == 0 and first),
                                            stop=False,
                                            tile_position=(32 * jj, 0))
                                    first = False
                            if w == 0:
                                te.matmul(ps[jj][:, 128:256],
                                          tbl["c_id"][:], tbl["c_m12"][:, 128:256],
                                          start=False, stop=False)
                                te.matmul(ps[jj][:, 384:512],
                                          tbl["c_id"][:], tbl["c_m12"][:, 384:512],
                                          start=False, stop=True
                                          ).then_inc(sems["c_sim"], 1)
                            else:
                                te.matmul(ps[jj][:], tbl["c_id"][:],
                                          tbl["c_m12"][:],
                                          start=False, stop=True
                                          ).then_inc(sems["c_sim"], 1)
                        W(te, "c_dv", w)
                        if "av" in KNO:
                            W(te, "c_exp", 8 * w + 4 * r + 4)
                            te.sem_inc(sems["c_avr"], 1)
                            continue
                        last = None
                        for hp in (0, 4, 1, 5, 2, 6, 3, 7):
                            if hp < 4:
                                W(te, "c_exp", 8 * w + 4 * r + hp + 1)
                            head = 8 * r + hp
                            bank = 4 + 2 * r + hp // 4
                            col0 = 65 * (hp % 4)
                            halves = (1,) if w == 0 else (0, 1)
                            for half in halves:
                                T = w - 1 + half
                                if T // 4 == b:
                                    rhsv = vpr[b % 2][:, T % 4,
                                                      65 * head:65 * head + 65]
                                else:
                                    rhsv = vprev[(b - 1) % 2][:,
                                                              65 * head:65 * head + 65]
                                last = te.matmul(
                                    ps[bank][:, col0:col0 + 65],
                                    abuf[gr % 2][:, hp % 4,
                                                 128 * (2 * (hp // 4) + half):
                                                 128 * (2 * (hp // 4) + half) + 128],
                                    rhsv,
                                    start=(half == halves[0]), stop=(half == 1))
                        last.then_inc(sems["c_avr"], 1)
    @block.vector
    def _(v):
        ins = None
        for rr in range(2):
            ones_ap = vpr[rr][:].rearrange("p s (h c) -> p (s h) c", c=65)[:, :, 64:65]
            ins = v.memset(ones_ap, 1.0)
        v.memset(cbias[:, 0:1], EPS)
        v.memset(cbias[:, 1:2], EXPB)
        DRI(v, "c_init")
        W(v, "s_cst", 16 * 6)
        for sg in range(-1, KLIM):
            if 0 <= sg < NS:
                if KDBG and sg == 0:
                    W(v, "s_dbg", 16)
                if KDBG and sg == 2:
                    W(v, "s_dbg", 96)
                for pair in range(8):
                    gi = 16 * sg + 2 * pair
                    W(v, "c_qk", gi + 2)
                    isq = pair < 4
                    dst = qro[sg % 2] if isq else khi[sg % 2]
                    cos = tbl["c_cosq"]
                    sin = tbl["c_sinq"]
                    cch = pair % 4
                    fa = ps[(2 * pair) % 6][:]
                    sa = ps[(2 * pair + 1) % 6][:]
                    cosb = cos[:].unsqueeze(1).to_broadcast([128, 4, 128])
                    sinb = sin[:].unsqueeze(1).to_broadcast([128, 4, 128])

                    def t4(ap):
                        return ap.rearrange("p (a n) -> p a n", n=128)
                    v.tensor_tensor(t4(dst[:, cch, :]), t4(fa), cosb,
                                    op=ALU.mult)
                    v.tensor_tensor(t4(dst[:, cch + 4, :]), t4(sa), cosb,
                                    op=ALU.mult)
                    v.tensor_tensor(t4(rs1[:]), t4(sa), sinb, op=ALU.mult)
                    v.tensor_tensor(t4(rs2[:]), t4(fa), sinb, op=ALU.mult)
                    DR(v)
                    v.tensor_sub(dst[:, cch, :], dst[:, cch, :], rs1[:])
                    v.tensor_add(dst[:, cch + 4, :], dst[:, cch + 4, :], rs2[:])
                    DRI(v, "c_rope")
                cosd = tbl["c_dl"][:, 0:1]
                sind = tbl["c_dl"][:, 1:2]
                for phase in range(2):
                    for pair in range(4):
                        for sl2 in range(4):
                            tgl = 4 * sg - 1 + sl2
                            if tgl < 0 or tgl >= NT:
                                continue
                            src = khi[(tgl // 4) % 2]
                            ssl = tgl % 4
                            f_ = src[:, pair, 128 * ssl:128 * (ssl + 1)]
                            s_ = src[:, pair + 4, 128 * ssl:128 * (ssl + 1)]
                            of = klo[sg % 2][:, pair, 128 * sl2:128 * (sl2 + 1)]
                            os_ = klo[sg % 2][:, pair + 4,
                                              128 * sl2:128 * (sl2 + 1)]
                            if phase == 0:
                                v.tensor_scalar(of, s_, sind, None, op0=ALU.mult)
                                v.tensor_scalar(os_, f_, sind, None, op0=ALU.mult)
                            else:
                                v.scalar_tensor_tensor(of, f_, cosd, of,
                                                       op0=ALU.mult, op1=ALU.add)
                                v.scalar_tensor_tensor(os_, s_, cosd, os_,
                                                       op0=ALU.mult,
                                                       op1=ALU.subtract)
                    DR(v)
                DRI(v, "c_klo")
            s1 = sg + 1
            if 0 <= s1 < NS:
                for hf in range(2):
                    p1 = 2 * s1 + hf
                    stb = st1[p1 % 2]
                    for i2 in range(2):
                        t = 4 * s1 + 2 * hf + i2
                        W(v, f"s_x{t % 2}", 16 * (t // 2 + 1))
                        xx = x_sb[t % X_RING][:]
                        v.bn_stats(stb[:, i2, 0:2, :].rearrange("p a b -> p (a b)"),
                                   xx[:, 0:512])
                        v.bn_stats(stb[:, i2, 2:4, :].rearrange("p a b -> p (a b)"),
                                   xx[:, 512:1024])
                    ln_aggr(v, stb, 2, ag1[p1 % 2]).then_inc(sems["c_ln1s"], 1)
                    W(v, "c_r1", p1 + 1)
                    v.reciprocal(ag1[p1 % 2][:, 2:4], ag1[p1 % 2][:, 2:4])
                    DR(v)
                    v.tensor_mul(ag1[p1 % 2][:, 4:6], ag1[p1 % 2][:, 4:6], ag1[p1 % 2][:, 2:4])
                    DRI(v, "c_nmr")
            bf = sg - 3
            bo = sg - 2
            for hf in range(2):
                if 0 <= bf < NS:
                    pairt = [4 * bf + 2 * hf, 4 * bf + 2 * hf + 1]
                    for i2, t in enumerate(pairt):
                        W(v, "c_ff", t + 1)
                        pf = (ps[4], ps[5]) if t % 2 == 0 else (ps[6], ps[7])
                        v.bn_stats(st4[:, i2, 0:2, :].rearrange("p a b -> p (a b)"),
                                   pf[0][:])
                        v.bn_stats(st4[:, i2, 2:4, :].rearrange("p a b -> p (a b)"),
                                   pf[1][:])
                    ln_aggr(v, st4, 2, ag4).then_inc(sems["c_l4g"], 1)
                if 0 <= bo < NS:
                    pairw = [4 * bo + 2 * hf, 4 * bo + 2 * hf + 1]
                    for i2, t in enumerate(pairw):
                        W(v, "c_wo", t + 1)
                        pw = (ps[0], ps[1]) if t % 2 == 0 else (ps[2], ps[3])
                        v.bn_stats(st2[:, i2, 0:2, :].rearrange("p a b -> p (a b)"),
                                   pw[0][:])
                        v.bn_stats(st2[:, i2, 2:4, :].rearrange("p a b -> p (a b)"),
                                   pw[1][:])
                    ln_aggr(v, st2, 2, ag2).then_inc(sems["c_l2g"], 1)
                if 0 <= bf < NS:
                    W(v, "c_r4", 2 * bf + hf + 1)
                    v.reciprocal(ag4[:, 2:4], ag4[:, 2:4])
                    DR(v)
                    v.tensor_mul(ag4[:, 4:6], ag4[:, 4:6], ag4[:, 2:4])
                    DR(v)
                    for i2, t in enumerate(pairt):
                        pf = (ps[4], ps[5]) if t % 2 == 0 else (ps[6], ps[7])
                        rstd = ag4[:, 2 + i2:3 + i2]
                        nmr = ag4[:, 4 + i2:5 + i2]
                        v.scalar_tensor_tensor(x1[t % 4][:, 0:512], pf[0][:],
                                               rstd, x1[t % 4][:, 0:512],
                                               op0=ALU.mult, op1=ALU.add)
                        v.scalar_tensor_tensor(x1[t % 4][:, 512:1024], pf[1][:],
                                               rstd, x1[t % 4][:, 512:1024],
                                               op0=ALU.mult, op1=ALU.add)
                        DR(v)
                        v.tensor_scalar(x1[t % 4][:, 0:512],
                                        x1[t % 4][:, 0:512], nmr, None,
                                        op0=ALU.add)
                        v.tensor_scalar(x1[t % 4][:, 512:1024],
                                        x1[t % 4][:, 512:1024], nmr, None,
                                        op0=ALU.add)
                        DRI(v, "c_oadd")
                if 0 <= bo < NS:
                    W(v, "c_r2", 2 * bo + hf + 1)
                    v.reciprocal(ag2[:, 2:4], ag2[:, 2:4])
                    DR(v)
                    v.tensor_mul(ag2[:, 4:6], ag2[:, 4:6], ag2[:, 2:4])
                    DRI(v, "c_n2")
                    for i2, t in enumerate(pairw):
                        W(v, "c_l2a", t + 1)
                        W(v, f"s_xr{t % 2}", 16 * (t // 2 + 1))
                        v.tensor_add(x1[t % 4][:], x1[t % 4][:],
                                     xr[t % XR_RING][:])
                        DRI(v, "c_x1")
            bo3 = sg - 2

            def ln3_stats(hf):
                if not (0 <= bo3 < NS):
                    return
                pairt = [4 * bo3 + 2 * hf, 4 * bo3 + 2 * hf + 1]
                W(v, "c_l3a", 4 * bo3 + 2 * hf)
                for i2, t in enumerate(pairt):
                    v.bn_stats(st3[:, i2, 0:2, :].rearrange("p a b -> p (a b)"),
                               x1[t % 4][:, 0:512])
                    v.bn_stats(st3[:, i2, 2:4, :].rearrange("p a b -> p (a b)"),
                               x1[t % 4][:, 512:1024])
                ln_aggr(v, st3, 2, ag3).then_inc(sems["c_l3g"], 1)

            def ln3_nmr(hf):
                if not (0 <= bo3 < NS):
                    return
                W(v, "c_r3", 2 * bo3 + hf + 1)
                v.reciprocal(ag3[:, 2:4], ag3[:, 2:4])
                DR(v)
                v.tensor_mul(ag3[:, 4:6], ag3[:, 4:6], ag3[:, 2:4])
                DR(v)
                pairt = [4 * bo3 + 2 * hf, 4 * bo3 + 2 * hf + 1]
                for i2, t in enumerate(pairt):
                    if t >= 2:
                        W(v, f"s_3t{(t - 2) % 4}", 16 * ((t - 2) // 4 + 1))
                    v.tensor_scalar(ln3bf[t % 2][:], x1[t % 4][:],
                                    ag3[:, 2 + i2:3 + i2],
                                    ag3[:, 4 + i2:5 + i2],
                                    op0=ALU.mult, op1=ALU.add)
                    DRI(v, "c_l3a")

            bb = sg - 1
            if not (0 <= bb < NS):
                ln3_stats(0)
                ln3_nmr(0)
                ln3_stats(1)
                ln3_nmr(1)
            if 0 <= bb < NS:
                for w in range(4 * bb, 4 * bb + 4):
                    if KDBG and w == 2:
                        W(v, "s_dbg", 96)
                    W(v, "c_avr", 2 * w + 2)
                    if w >= 2:
                        W(v, f"s_at{(w - 2) % 4}", 16 * ((w - 2) // 4 + 1))
                    if "div" in KNO:
                        v.sem_inc(sems["c_dv"], 1)
                        continue
                    for ab in range(4):
                        pden = ps[4 + ab][:, 0:260].rearrange(
                            "p (h c) -> p h c", c=65)
                        v.reciprocal(
                            rden[:, 4 * ab:4 * ab + 4],
                            pden[:, :, 64:65].rearrange("p a b -> p (a b)"))
                    DR(v)
                    for ab in range(4):
                        pden = ps[4 + ab][:, 0:260].rearrange(
                            "p (h c) -> p h c", c=65)
                        h0 = 8 * (ab // 2) + 4 * (ab % 2)
                        v.tensor_tensor(
                            ao[w % AO_RING][:, 64 * h0:64 * h0 + 256].rearrange(
                                "p (h c) -> p h c", c=64),
                            pden[:, :, 0:64],
                            rden[:, 4 * ab:4 * ab + 4].unsqueeze(2).to_broadcast(
                                [128, 4, 64]),
                            op=ALU.mult)
                    DRI(v, "c_dv")
                    if w == 4 * bb + 1:
                        ln3_stats(0)
                    elif w == 4 * bb + 2:
                        ln3_nmr(0)
                    elif w == 4 * bb + 3:
                        ln3_stats(1)
            if 0 <= sg < NS:
                W(v, "c_ve", 4 * sg + 4)
                W(v, "c_avr", 8 * sg - 6)
                v.tensor_copy(vprev[sg % 2][:], vpr[sg % 2][:, 3, :])
                DRI(v, "c_vprev")
            if 0 <= bb < NS:
                ln3_nmr(1)

    @block.scalar
    def _(a):
        W(a, "c_mbT", 1)
        a.activation(out=t12a[:, 0:512], in_=ps[0][:], func=AF.Copy)
        a.activation(out=t12a[:, 512:1024], in_=ps[1][:], func=AF.Copy)
        DRI(a, "c_mbe")
        W(a, "c_mbT", 2)
        a.activation(out=t12b[:, 0:512], in_=ps[2][:], func=AF.Copy, scale=-1.0)
        a.activation(out=t12b[:, 512:1024], in_=ps[3][:], func=AF.Copy, scale=-1.0)
        DRI(a, "c_mbe")
        W(a, "c_mbM", 1)
        a.activation(out=mslice[:, 0:512], in_=ps[4][:], func=AF.Copy,
                     scale=1.0 / 1024.0)
        a.activation(out=mslice[:, 512:1024], in_=ps[5][:], func=AF.Copy,
                     scale=1.0 / 1024.0)
        DRI(a, "c_mbe")

        for sg in range(-1, KLIM):
            if 0 <= sg < NS:
                for tt in range(SUP):
                    W(a, "c_v", 4 * sg + tt + 1)
                    W(a, "c_init", 1)
                    vdst = vpr[sg % 2][:, tt, :].rearrange("p (h c) -> p h c", c=65)
                    a.activation(out=vdst[:, 0:8, 0:64], in_=ps[6][:].rearrange(
                        "p (h c) -> p h c", c=64), func=AF.Copy)
                    a.activation(out=vdst[:, 8:16, 0:64],
                                 in_=ps[7][:].rearrange(
                                     "p (h c) -> p h c", c=64), func=AF.Copy
                                 ).then_inc(sems["c_ve"], 1)
            s1 = sg + 1
            if 0 <= s1 < NS:
                for hf in range(2):
                    p1 = 2 * s1 + hf
                    ag = ag1[p1 % 2]
                    W(a, "c_ln1s", p1 + 1)
                    a.activation(out=ag[:, 2:4], in_=ag[:, 2:4],
                                 func=AF.Sqrt)
                    DRI(a, "c_r1")
                    W(a, "c_nmr", p1 + 1)
                    for i2 in range(2):
                        t = 4 * s1 + 2 * hf + i2
                        a.activation(out=x_sb[t % X_RING][:],
                                     in_=x_sb[t % X_RING][:],
                                     func=AF.Identity, scale=ag[:, 2 + i2:3 + i2],
                                     bias=ag[:, 4 + i2:5 + i2]
                                     ).then_inc(sems["c_l1a"], 1)
            bf = sg - 3
            bo = sg - 2
            for hf in range(2):
                if 0 <= bf < NS:
                    W(a, "c_l4g", 2 * bf + hf + 1)
                    a.activation(out=ag4[:, 2:4], in_=ag4[:, 2:4],
                                 func=AF.Sqrt)
                    DRI(a, "c_r4")
                if 0 <= bo < NS:
                    pairt = [4 * bo + 2 * hf, 4 * bo + 2 * hf + 1]
                    W(a, "c_l2g", 2 * bo + hf + 1)
                    a.activation(out=ag2[:, 2:4], in_=ag2[:, 2:4],
                                 func=AF.Sqrt)
                    DRI(a, "c_r2")
                    W(a, "c_n2", 2 * bo + hf + 1)
                    for i2, t in enumerate(pairt):
                        W(a, "c_oadd", t - 3)
                        W(a, f"s_out{t % 2}", 16 * (t // 2 - 1))
                        pw = (ps[0], ps[1]) if t % 2 == 0 else (ps[2], ps[3])
                        a.activation(out=x1[t % 4][:, 0:512], in_=pw[0][:],
                                     func=AF.Identity, scale=ag2[:, 2 + i2:3 + i2],
                                     bias=ag2[:, 4 + i2:5 + i2])
                        a.activation(out=x1[t % 4][:, 512:1024], in_=pw[1][:],
                                     func=AF.Identity, scale=ag2[:, 2 + i2:3 + i2],
                                     bias=ag2[:, 4 + i2:5 + i2]
                                     ).then_inc(sems["c_l2a"], 1)
            bo = sg - 2

            def act_sqrt3(hf):
                if not (0 <= bo < NS):
                    return
                W(a, "c_l3g", 2 * bo + hf + 1)
                a.activation(out=ag3[:, 2:4], in_=ag3[:, 2:4], func=AF.Sqrt)
                DRI(a, "c_r3")

            bb = sg - 1
            if 0 <= bb < NS:
                for w in range(4 * bb, 4 * bb + 4):
                    for r in range(2):
                        gr = 2 * w + r
                        if KDBG and gr == 2:
                            W(a, "s_dbg", 96)
                        for sb in range(4):
                            W(a, "c_sim", 8 * w + 4 * r + sb + 1)
                            W(a, "c_avr", gr - 1)
                            if "exp" in KNO:
                                a.sem_inc(sems["c_exp"], 1)
                                continue
                            if w == 0:
                                src = ps[sb][:].rearrange(
                                    "p (s n) -> p s n", n=128)[:, 1:4:2, :]
                                dst = abuf[gr % 2][:, sb, :].rearrange(
                                    "p (s n) -> p s n", n=128)[:, 1:4:2, :]
                            else:
                                src = ps[sb][:]
                                dst = abuf[gr % 2][:, sb, :]
                            a.activation(out=dst, in_=src, func=AF.Exp,
                                         scale=8.0, bias=cbias[:, 1:2]
                                         ).then_inc(sems["c_exp"], 1)
                    if w == 4 * bb + 1:
                        act_sqrt3(0)
            bo = sg - 2
            if 0 <= bo < NS and not (0 <= sg - 1 < NS):
                act_sqrt3(0)
            if 0 <= bo < NS:
                act_sqrt3(1)
    blkmgr.__exit__(None, None, None)
    for cm in reversed(_ctxs):
        cm.__exit__(None, None, None)
    return nc, SB_REM


def kernel(L0_attn_x, L0_attn_W_qkv, L0_attn_W_out,
           L0_ffn_w_freq_real, L0_ffn_w_freq_imag, L0_ffn_bias):
    x = np.asarray(L0_attn_x, dtype=np.float32)
    Wqkv = np.asarray(L0_attn_W_qkv, dtype=np.float32)
    Wout = np.ascontiguousarray(
        np.asarray(L0_attn_W_out, dtype=np.float32).astype(ml_dtypes.bfloat16))
    wfr = np.ascontiguousarray(
        np.asarray(L0_ffn_w_freq_real, dtype=np.float32).astype(ml_dtypes.bfloat16))
    wfi = np.ascontiguousarray(
        np.asarray(L0_ffn_w_freq_imag, dtype=np.float32).astype(ml_dtypes.bfloat16))
    fbias = np.ascontiguousarray(
        np.asarray(L0_ffn_bias, dtype=np.float32).reshape(1, D)
        .astype(ml_dtypes.bfloat16))

    perm = _quad_perm()
    subs = []
    for i in range(16):
        which = 0 if i < 8 else 1
        c = CHUNK_ORDER[i % 8]
        cols = which * D + perm[c * 128:(c + 1) * 128]
        for kk in range(8):
            subs.append(np.ascontiguousarray(Wqkv[128 * kk:128 * (kk + 1), cols]))
    wqk_host = np.stack(subs, axis=0).astype(ml_dtypes.bfloat16)

    wv_host = np.ascontiguousarray(Wqkv[:, 2 * D:3 * D]).astype(ml_dtypes.bfloat16)
    cst = _host_constants()

    nc, _ = build_nc()

    grid = (2.0 * np.pi / D) * ((np.arange(D)[:, None] * np.arange(D)[None, :]) % D)
    Cd = np.cos(grid)
    Sd = np.sin(grid)

    in_maps = []
    for r in range(NCORE):
        dsub = np.ascontiguousarray(np.stack([
            Cd[:, 128 * r:128 * (r + 1)],
            Sd[:, 128 * r:128 * (r + 1)],
            -Sd[:, 128 * r:128 * (r + 1)],
        ]).astype(ml_dtypes.bfloat16))
        im = {
            "xin": np.ascontiguousarray(x[r]),
            "wqk": wqk_host,
            "wv": wv_host,
            "wo": Wout,
            "wfr": wfr,
            "wfi": wfi,
            "fbias": fbias,
            "dsub": dsub,
            "c_dftC": cst["c_dftC"],
            "c_dftS": cst["c_dftS"],
        }
        for nm in CONST_NAMES:
            im[nm] = cst[nm]
        in_maps.append(im)

    res = run_bass_kernel_spmd(nc, in_maps, core_ids=list(range(NCORE)))
    out = np.stack([res.results[r]["yout"] for r in range(NCORE)], axis=0)
    if os.environ.get("KDBG", "0") == "1":
        kernel.dbg_results = res.results
    return out


# version 32
# speedup vs baseline: 1.1586x; 1.1371x over previous
import os
import numpy as np
import ml_dtypes
import concourse.bass as bass
import concourse.mybir as mybir
from concourse.bass_utils import run_bass_kernel_spmd

F32 = mybir.dt.float32
BF16 = mybir.dt.bfloat16
AF = mybir.ActivationFunctionType
ALU = mybir.AluOpType
AX = mybir.AxisListType

NCORE = 8
SEQ = 4096
D = 1024
H = 16
DH = 64
NT = 32
SUP = 4
NS = 8
KC = 8
EPS = 1e-5
EXPB = -8.0
MASKV = -30000.0

X_RING = 2
AO_RING = 2
AOT_RING = 4
XR_RING = 2

CHUNK_ORDER = [0, 4, 1, 5, 2, 6, 3, 7]
WQK_COVER = {}
WQK_CNT = {0: 0, 1: 0, 2: 0, 3: 0}
KLIM = int(os.environ.get("KLIM", str(NS + 3)))
KNO = set(os.environ.get("KNO", "").split(",")) - {""}
KDBG = os.environ.get("KDBG", "0") == "1"
KDRAIN = os.environ.get("KDRAIN", "1") == "1"
NSTORE = sum(1 for sg in range(KLIM) for t in range(4) if 0 <= sg - 3 < NS)


def _quad_perm():
    perm = np.zeros(1024, dtype=np.int64)
    for c in range(8):
        half = c // 4
        for p in range(128):
            head = 4 * (c % 4) + p // 32
            d = (p % 32) + 32 * half
            perm[c * 128 + p] = head * DH + d
    return perm


def _host_constants():
    cst = {}
    inv = 10000.0 ** (-(np.arange(32) / 32.0))
    t = np.arange(128)
    idx = np.arange(128) % 32
    phq = (128.0 + t)[None, :] * inv[idx][:, None]

    def tile4(a):
        return np.ascontiguousarray(np.tile(a, (1, SUP))).astype(ml_dtypes.bfloat16)

    cst["c_cosq"] = np.ascontiguousarray(np.cos(phq) * 0.125).astype(
        ml_dtypes.bfloat16)
    cst["c_sinq"] = np.ascontiguousarray(np.sin(phq) * 0.125).astype(
        ml_dtypes.bfloat16)
    j = np.arange(128)[:, None]
    q = np.arange(128)[None, :]
    m1 = np.where(j < q, MASKV, 0.0)
    m2 = np.where(j > q, MASKV, 0.0)
    cst["c_m12"] = np.ascontiguousarray(
        np.concatenate([m1, m2, m1, m2], axis=1)).astype(ml_dtypes.bfloat16)
    cst["c_id"] = np.eye(128, dtype=ml_dtypes.bfloat16)
    cst["c_ones"] = np.ones((1, 128), dtype=ml_dtypes.bfloat16)
    delta = 128.0 * inv[idx]
    cst["c_dl"] = np.stack([np.cos(delta), np.sin(delta)], axis=1).astype(np.float32)
    grid = (2.0 * np.pi / D) * ((np.arange(D)[:, None] * np.arange(D)[None, :]) % D)
    cst["c_dftC"] = np.cos(grid).astype(ml_dtypes.bfloat16)
    cst["c_dftS"] = np.sin(grid).astype(ml_dtypes.bfloat16)
    return cst


CONST_NAMES = ["c_cosq", "c_sinq", "c_m12", "c_id", "c_ones", "c_dl"]
_SHAPES = {"c_id": [128, 128], "c_ones": [1, 128], "c_dl": [128, 2],
           "c_cosq": [128, 128], "c_sinq": [128, 128]}


def build_nc():
    nc = bass.Bass("TRN2", target_bir_lowering=False)

    xin = nc.dram_tensor("xin", [SEQ, D], F32, kind="ExternalInput").ap()
    wqk = nc.dram_tensor("wqk", [128, 128, 128], BF16, kind="ExternalInput").ap()
    wv = nc.dram_tensor("wv", [D, D], BF16, kind="ExternalInput").ap()
    wo = nc.dram_tensor("wo", [D, D], BF16, kind="ExternalInput").ap()
    wfr = nc.dram_tensor("wfr", [D, D], BF16, kind="ExternalInput").ap()
    wfi = nc.dram_tensor("wfi", [D, D], BF16, kind="ExternalInput").ap()
    fbias = nc.dram_tensor("fbias", [1, D], BF16, kind="ExternalInput").ap()
    dsub = nc.dram_tensor("dsub", [3, D, 128], BF16, kind="ExternalInput").ap()
    c_in = {}
    for nm in CONST_NAMES:
        shp = _SHAPES.get(nm, [128, 512])
        dt = F32 if nm == "c_dl" else BF16
        c_in[nm] = nc.dram_tensor(nm, shp, dt, kind="ExternalInput").ap()
    c_dftC = nc.dram_tensor("c_dftC", [D, D], BF16, kind="ExternalInput").ap()
    c_dftS = nc.dram_tensor("c_dftS", [D, D], BF16, kind="ExternalInput").ap()
    yout = nc.dram_tensor("yout", [SEQ, D], F32, kind="ExternalOutput").ap()
    dbg = {}
    if KDBG:
        for nm, shp in (("d_ln1t", [128, KC, 512]), ("d_qkst", [128, 2, 512]),
                        ("d_qro", [128, KC, 512]), ("d_khi", [128, KC, 512]),
                        ("d_klo", [128, KC, 384]), ("d_abuf", [128, 4, 2, 128]),
                        ("d_ao", [128, D])):
            dbg[nm] = nc.dram_tensor(nm, shp, BF16, kind="ExternalOutput").ap()

    ccin = nc.dram_tensor("ccin", [128, D], F32, kind="Internal")
    ccout = nc.dram_tensor("ccout", [D, D], F32, kind="Internal")

    A = nc.alloc_sbuf_tensor
    wv_sb = A("wv_sb", [128, KC, D], BF16)
    wo_sb = A("wo_sb", [128, KC, D], BF16)
    m_sb = A("m_sb", [128, KC, D], BF16)
    wqk_flat = A("wqk_sb", [128, 32, 128], BF16)
    wqk_sb = [wqk_flat[:][:, 8 * i:8 * (i + 1), :] for i in range(4)]
    x_sb = [A(f"x_sb{i}", [128, D], BF16) for i in range(X_RING)]
    ln1T = [A(f"ln1T{i}", [128, KC, 512], BF16, is_transpose_dest=True)
            for i in range(2)]
    qro = [A(f"qro{i}", [128, KC, 512], BF16) for i in range(2)]
    khi = [A(f"khi{i}", [128, KC, 512], BF16) for i in range(2)]
    klo = [A(f"klo{i}", [128, KC, 512], BF16) for i in range(2)]
    vpr = [A(f"vpr{i}", [128, SUP, H * 65], BF16) for i in range(2)]
    vprev = [A(f"vprev{i}", [128, H * 65], BF16) for i in range(2)]
    abuf = [A(f"abuf{i}", [128, 4, 512], BF16) for i in range(2)]
    ao = [A(f"ao{i}", [128, D], BF16) for i in range(AO_RING)]
    aoT = [A(f"aoT{i}", [128, KC, 128], BF16, is_transpose_dest=True)
           for i in range(AOT_RING)]
    xr = [A(f"xr{i}", [128, D], F32) for i in range(XR_RING)]
    x1 = [A(f"x1{i}", [128, D], F32) for i in range(4)]
    ln3bf = [A(f"ln3bf{i}", [128, D], BF16) for i in range(2)]
    ln3T = [A(f"ln3T{i}", [128, KC, 128], BF16, is_transpose_dest=True)
            for i in range(4)]
    biasT = A("biasT", [1, D], BF16)
    tbl = {}
    for nm in CONST_NAMES:
        shp = _SHAPES.get(nm, [128, 512])
        dt = F32 if nm == "c_dl" else BF16
        tbl[nm] = A("t" + nm[1:], shp, dt)
    st1 = [A(f"st1_{i}", [128, 2, 4, 3], F32) for i in range(2)]
    ag1 = [A(f"ag1_{i}", [128, 20], F32) for i in range(2)]
    st2 = A("st2", [128, 2, 4, 3], F32)
    ag2 = A("ag2", [128, 20], F32)
    st3 = st1[0]
    ag3 = A("ag3", [128, 20], F32)
    st4 = A("st4", [128, 2, 4, 3], F32)
    ag4 = A("ag4", [128, 20], F32)
    rden = A("rden", [128, 16], F32)
    rs1 = A("rs1", [128, 512], BF16)
    rs2 = A("rs2", [128, 512], BF16)
    cbias = A("cbias", [128, 2], F32)

    def chunks4(ap3d):
        return ap3d.rearrange("p c n -> p (c n)").rearrange("p (c n) -> p c n", c=4)
    wr_al = [chunks4(qro[0][:]), chunks4(qro[1][:])]
    wi_al = [chunks4(khi[0][:]), chunks4(khi[1][:])]
    dsub_al = klo[0][:].rearrange("p c n -> p (c n)")[:, 0:24 * 128].rearrange(
        "p (c n) -> p c n", c=24)
    abuf0f = abuf[0][:].rearrange("p c n -> p (c n)")
    abuf1f = abuf[1][:].rearrange("p c n -> p (c n)")
    crow_dst = [ao[0][:], ao[1][:], abuf0f[:, D:2 * D], abuf1f[:, D:2 * D],
                vprev[0][:][:, 0:D], vprev[1][:][:, 0:D],
                x1[0][:].bitcast(BF16)[:, 0:D]]
    crow_lo = [c[:, 0:512] for c in crow_dst] + [rs1[:]]
    crow_hi = [c[:, 512:1024] for c in crow_dst] + [rs2[:]]
    srow_al = m_sb[:]
    t12a = abuf[0][:].rearrange("p c n -> p (c n)")[:, 0:D]
    t12b = abuf[1][:].rearrange("p c n -> p (c n)")[:, 0:D]
    t1T_al = aoT[0]
    t2T_al = aoT[1]
    mslice = xr[0][:]

    SB_REM = nc.sbuf_bytes_remaining
    ps = [nc.alloc_psum_tensor(f"ps{i}", [128, 512], F32) for i in range(8)]

    semnames = [
        "s_x0", "s_x1", "s_wqk0", "s_wqk1", "s_wqk2", "s_wqk3", "s_cst", "s_lt0", "s_lt1",
        "s_at0", "s_at1", "s_at2", "s_at3", "s_3t0", "s_3t1", "s_3t2", "s_3t3",
        "s_xr0", "s_xr1", "s_out0", "s_out1",
        "s_mbo", "s_cc", "s_mld", "s_mbi", "s_mb2", "s_mbt", "s_dbg", "s_wo", "s_bias", "s_wv",
        "c_qk", "c_v", "c_sim", "c_avr", "c_wo", "c_ff", "c_mbT", "c_mbM",
        "c_init", "c_ln1s", "c_nmr", "c_rope", "c_klo", "c_dv", "c_vprev",
        "c_l2g", "c_n2", "c_x1", "c_l3g", "c_n3", "c_l4g", "c_n4", "c_oadd",
        "c_r1", "c_l1a", "c_qke", "c_ve", "c_exp",
        "c_r2", "c_l2a", "c_r3", "c_l3a", "c_r4", "c_l4a", "c_mbe",
    ]
    sems = {}
    _ctxs = []
    for n in semnames:
        cm = nc.semaphore(n)
        sems[n] = cm.__enter__()
        _ctxs.append(cm)

    blkmgr = nc.Block()
    block = blkmgr.__enter__()

    def W(e, sem, val):
        if val > 0:
            e.wait_ge(sems[sem], val)

    def DR(e):
        if KDRAIN:
            e.drain()

    def DRI(e, sem, n=1):
        if KDRAIN:
            e.drain().then_inc(sems[sem], n)
        else:
            e.engine_nop().then_inc(sems[sem], n)

    def ln_aggr(v, stbuf, nt, wk):
        m_c = wk[:, 0:nt]
        var_c = wk[:, nt:2 * nt]
        negm_c = wk[:, 2 * nt:3 * nt]
        tmp_c = wk[:, 3 * nt:4 * nt]
        tmp2_c = wk[:, 4 * nt:5 * nt]
        sq_c = wk[:, 5 * nt:9 * nt].rearrange("p (t g) -> p t g", g=4)
        means = stbuf[:, :, :, 1]
        m2s = stbuf[:, :, :, 2]
        DR(v)
        v.reduce_sum(tmp_c.unsqueeze(2), means, axis=AX.X)
        v.tensor_mul(sq_c, means, means)
        v.reduce_sum(var_c.unsqueeze(2), m2s, axis=AX.X)
        DR(v)
        v.tensor_scalar_mul(m_c, tmp_c, 0.25)
        v.tensor_scalar_mul(negm_c, tmp_c, -0.25)
        v.reduce_sum(tmp2_c.unsqueeze(2), sq_c, axis=AX.X)
        v.tensor_scalar_mul(var_c, var_c, 1.0 / 1024.0)
        DR(v)
        v.tensor_mul(tmp_c, m_c, m_c)
        v.scalar_tensor_tensor(var_c, tmp2_c, 0.25, var_c, op0=ALU.mult,
                               op1=ALU.add)
        DR(v)
        v.tensor_sub(var_c, var_c, tmp_c)
        DR(v)
        v.tensor_scalar(var_c, var_c, EPS, None, op0=ALU.add)
        if KDRAIN:
            return v.drain()
        return v.engine_nop()

    @block.gpsimd
    def _(g):
        def load_x(t):
            if not (0 <= t < NT) or t // 4 > KLIM:
                return
            if t >= 2:
                W(g, f"s_lt{t % 2}", 16 * (t // 2))
            g.dma_start(out=x_sb[t % X_RING][:],
                        in_=xin[128 * t:128 * (t + 1), :]
                        ).then_inc(sems[f"s_x{t % 2}"], 16)

        for nm in CONST_NAMES:
            g.dma_start(out=tbl[nm][:], in_=c_in[nm]).then_inc(sems["s_cst"], 16)
        wfr4 = wfr.rearrange("(c p) n -> p c n", p=128)
        wfi4 = wfi.rearrange("(c p) n -> p c n", p=128)
        g.dma_start(out=wr_al[0], in_=wfr4[:, 0:4, :]).then_inc(sems["s_mbi"], 16)
        g.dma_start(out=wr_al[1], in_=wfr4[:, 4:8, :]).then_inc(sems["s_mbi"], 16)
        g.dma_start(out=wi_al[0], in_=wfi4[:, 0:4, :]).then_inc(sems["s_mbi"], 16)
        g.dma_start(out=wi_al[1], in_=wfi4[:, 4:8, :]).then_inc(sems["s_mbi"], 16)
        g.dma_start(out=dsub_al,
                    in_=dsub.rearrange("s (c p) n -> p (s c) n", p=128)
                    ).then_inc(sems["s_mbi"], 16)
        dc4 = c_dftC.rearrange("(c p) n -> p c n", p=128)
        ds4 = c_dftS.rearrange("(c p) n -> p c n", p=128)
        for e in range(7):
            g.dma_start(out=crow_dst[e],
                        in_=dc4[:, e:e + 1, :].rearrange("p c n -> p (c n)")
                        ).then_inc(sems["s_mb2"], 16)
        g.dma_start(out=rs1[:], in_=dc4[:, 7:8, 0:512].rearrange(
            "p c n -> p (c n)")).then_inc(sems["s_mb2"], 16)
        g.dma_start(out=rs2[:], in_=dc4[:, 7:8, 512:1024].rearrange(
            "p c n -> p (c n)")).then_inc(sems["s_mb2"], 16)
        g.dma_start(out=srow_al, in_=ds4).then_inc(sems["s_mb2"], 16)
        for t in range(4):
            load_x(t)
        g.dma_start(out=wv_sb[:], in_=wv.rearrange("(c p) n -> p c n", p=128)
                    ).then_inc(sems["s_wv"], 16)
        if KDBG:
            W(g, "s_lt0", 32)
            W(g, "s_lt1", 32)
            g.dma_start(out=dbg["d_ln1t"], in_=ln1T[0][:]).then_inc(sems["s_dbg"], 16)
        for t in range(4, 8):
            load_x(t)
        g.dma_start(out=wo_sb[:], in_=wo.rearrange("(c p) n -> p c n", p=128)
                    ).then_inc(sems["s_wo"], 16)
        g.dma_start(out=biasT[:], in_=fbias).then_inc(sems["s_bias"], 16)
        W(g, "c_mbe", 3)
        g.dma_start(out=ccin.ap(), in_=mslice).then_inc(sems["s_mbo"], 16)
        W(g, "s_mbo", 16)
        g.collective_compute(
            "AllGather", ALU.bypass, replica_groups=[list(range(NCORE))],
            ins=[ccin.ap().opt()], outs=[ccout.ap().opt()],
        ).then_inc(sems["s_cc"])
        W(g, "s_cc", 1)
        g.dma_start(out=m_sb[:], in_=ccout.ap().rearrange("(c p) n -> p c n", p=128)
                    ).then_inc(sems["s_mld"], 16)

        for sg in range(KLIM):
            if KDBG and sg == 1:
                W(g, "c_rope", 8)
                g.dma_start(out=dbg["d_qro"], in_=qro[0][:]).then_inc(sems["s_dbg"], 16)
                g.dma_start(out=dbg["d_khi"], in_=khi[0][:]).then_inc(sems["s_dbg"], 16)
                W(g, "c_klo", 1)
                g.dma_start(out=dbg["d_klo"], in_=klo[0][:, :, 128:512]).then_inc(sems["s_dbg"], 16)
                W(g, "c_exp", 4)
                g.dma_start(out=dbg["d_abuf"], in_=abuf[0][:].rearrange("p s (t n) -> p s t n", n=128)[:, :, 1:4:2, :]).then_inc(sems["s_dbg"], 16)
                W(g, "c_dv", 1)
                g.dma_start(out=dbg["d_ao"], in_=ao[0][:]).then_inc(sems["s_dbg"], 16)
            def load_xr(t):
                W(g, "c_x1", t - 1)
                if t == 0:
                    W(g, "s_mbo", 16)
                g.dma_start(out=xr[t % XR_RING][:],
                            in_=xin[128 * t:128 * (t + 1), :]
                            ).then_inc(sems[f"s_xr{t % 2}"], 16)

            b2 = sg - 2
            if 0 <= b2 < NS and b2 <= KLIM - 3:
                load_xr(4 * b2 + 2)
                load_xr(4 * b2 + 3)
            bb = sg - 1
            if 0 <= bb < NS and bb <= KLIM - 3:
                load_xr(4 * bb)
                load_xr(4 * bb + 1)
            for tt in range(SUP):
                load_x(4 * (sg + 2) + tt)

    @block.sync
    def _(sy):
        WQK_COVER.clear()
        WQK_CNT.clear()
        WQK_CNT.update({0: 0, 1: 0, 2: 0, 3: 0})
        W(sy, "c_mbe", 1)
        sy.dma_start_transpose(t1T_al[:], t12a.rearrange("p (c n) -> p c n", c=KC)
                               ).then_inc(sems["s_mbt"], 16)
        W(sy, "c_mbe", 2)
        sy.dma_start_transpose(t2T_al[:], t12b.rearrange("p (c n) -> p c n", c=KC)
                               ).then_inc(sems["s_mbt"], 16)
        for sg in range(-1, KLIM):
            if 0 <= sg < NS:
                for grp in range(4, 16):
                    G = 16 * sg + grp
                    W(sy, "c_qk", G - 3)
                    sy.dma_start(
                        out=wqk_sb[G % 4][:],
                        in_=wqk[8 * grp:8 * (grp + 1), :, :].rearrange(
                            "c p n -> p c n"),
                    ).then_inc(sems[f"s_wqk{G % 4}"], 16)
                    WQK_COVER[G] = (G % 4, WQK_CNT[G % 4] + 1)
                    WQK_CNT[G % 4] += 1
            s1 = sg + 1
            if 0 <= s1 < NS:
                if KDBG and s1 == 2:
                    W(sy, "s_dbg", 16)
                for tt in range(SUP):
                    t = 4 * s1 + tt
                    W(sy, "c_l1a", t + 1)
                    sy.dma_start_transpose(
                        ln1T[s1 % 2][:, :, 128 * tt:128 * (tt + 1)],
                        x_sb[t % X_RING][:].rearrange("p (c n) -> p c n", c=KC),
                    ).then_inc(sems[f"s_lt{t % 2}"], 16)
            s1w = sg + 1
            if 0 <= s1w < NS:
                for grp in range(4):
                    G = 16 * s1w + grp
                    if G >= 4:
                        W(sy, "c_qk", G - 3)
                    sy.dma_start(
                        out=wqk_sb[G % 4][:],
                        in_=wqk[8 * grp:8 * (grp + 1), :, :].rearrange(
                            "c p n -> p c n"),
                    ).then_inc(sems[f"s_wqk{G % 4}"], 16)
                    WQK_COVER[G] = (G % 4, WQK_CNT[G % 4] + 1)
                    WQK_CNT[G % 4] += 1
            def ln3t_pairA():
                bo = sg - 2
                if not (0 <= bo < NS):
                    return
                for t in (4 * bo, 4 * bo + 1):
                    W(sy, "c_l3a", t + 1)
                    W(sy, "c_ff", t - 3)
                    sy.dma_start_transpose(
                        ln3T[t % 4][:],
                        ln3bf[t % 2][:].rearrange("p (c n) -> p c n", c=KC),
                    ).then_inc(sems[f"s_3t{t % 4}"], 16)

            bb = sg - 1
            if not (0 <= bb < NS):
                ln3t_pairA()
            if 0 <= bb < NS:
                for w in range(4 * bb, 4 * bb + 4):
                    if w == 4 * bb + 2:
                        ln3t_pairA()
                    W(sy, "c_dv", w + 1)
                    W(sy, "c_wo", w - 3)
                    if "aot" in KNO:
                        sy.dma_start(out=ccin.ap().bitcast(BF16)[0:128, 0:512],
                                     in_=ao[w % AO_RING][:, 0:512]
                                     ).then_inc(sems[f"s_at{w % 4}"], 16)
                        continue
                    sy.dma_start_transpose(
                        aoT[w % AOT_RING][:],
                        ao[w % AO_RING][:].rearrange("p (c n) -> p c n", c=KC),
                    ).then_inc(sems[f"s_at{w % 4}"], 16)
            bo = sg - 2
            if 0 <= bo < NS:
                for t in (4 * bo + 2, 4 * bo + 3):
                    W(sy, "c_l3a", t + 1)
                    W(sy, "c_ff", t - 3)
                    sy.dma_start_transpose(
                        ln3T[t % 4][:],
                        ln3bf[t % 2][:].rearrange("p (c n) -> p c n", c=KC),
                    ).then_inc(sems[f"s_3t{t % 4}"], 16)
        W(sy, "s_out0", 16 * NSTORE // 2)
        W(sy, "s_out1", 16 * NSTORE // 2)

    @block.tensor
    def _(te):
        W(te, "s_mbi", 16 * 5)
        for (pslo, pshi, rhs_a, rhs_b, suba, subb, csem) in (
                (ps[0], ps[1], wr_al, wi_al, 0, 8, 1),
                (ps[2], ps[3], wi_al, wr_al, 0, 16, 2)):
            last = None
            for e in range(8):
                lhs1 = dsub_al[:, suba + e, :]
                lhs2 = dsub_al[:, subb + e, :]
                r1 = rhs_a[e // 4][:, e % 4, :]
                r2 = rhs_b[e // 4][:, e % 4, :]
                st = (e == 0)
                te.matmul(pslo[:], lhs1, r1[:, 0:512], start=st, stop=False)
                te.matmul(pshi[:], lhs1, r1[:, 512:1024], start=st, stop=False)
                te.matmul(pslo[:], lhs2, r2[:, 0:512], start=False, stop=(e == 7))
                last = te.matmul(pshi[:], lhs2, r2[:, 512:1024],
                                 start=False, stop=(e == 7))
            last.then_inc(sems["c_mbT"], 1)
        W(te, "s_mbt", 32)
        W(te, "s_mb2", 16 * 10)
        last = None
        for e in range(8):
            sr = srow_al[:, e, :]
            st = (e == 0)
            te.matmul(ps[4][:], t1T_al[:, e, :], crow_lo[e], start=st, stop=False)
            te.matmul(ps[5][:], t1T_al[:, e, :], crow_hi[e], start=st, stop=False)
            te.matmul(ps[4][:], t2T_al[:, e, :], sr[:, 0:512],
                      start=False, stop=(e == 7))
            last = te.matmul(ps[5][:], t2T_al[:, e, :], sr[:, 512:1024],
                             start=False, stop=(e == 7))
        last.then_inc(sems["c_mbM"], 1)

        W(te, "s_cst", 16 * 6)
        W(te, "s_wv", 16)
        for sg in range(KLIM):
            if sg < NS:
                W(te, "s_lt0", 16 * (2 * sg + 2))
                W(te, "s_lt1", 16 * (2 * sg + 2))
                W(te, "c_l2a", 4 * (sg - 2))
                W(te, "c_oadd", 4 * (sg - 3))
                W(te, "c_exp", 32 * (sg - 1))
                W(te, "c_dv", 4 * (sg - 1))
                for i in range(16):
                    G = 16 * sg + i
                    cslot, ccnt = WQK_COVER[G]
                    W(te, f"s_wqk{cslot}", 16 * ccnt)
                    gi16 = G % 16
                    prevG = G - 6 if gi16 >= 6 else (G - 4 if gi16 <= 3 else G - 10)
                    if prevG >= 0:
                        W(te, "c_rope", prevG // 2 + 1)
                    bank = i % 6
                    last = None
                    for kk in range(8):
                        last = te.matmul(
                            ps[bank][:],
                            wqk_sb[G % 4][:, kk, :],
                            ln1T[sg % 2][:, kk, :],
                            start=(kk == 0), stop=(kk == 7))
                    last.then_inc(sems["c_qk"], 1)
                for tt in range(SUP):
                    W(te, "c_ve", 4 * sg + tt)
                    last = None
                    for kk in range(8):
                        lh = ln1T[sg % 2][:, kk, 128 * tt:128 * (tt + 1)]
                        te.matmul(ps[6][:], lh, wv_sb[:, kk, 0:512],
                                  start=(kk == 0), stop=(kk == 7))
                        last = te.matmul(ps[7][:], lh, wv_sb[:, kk, 512:1024],
                                         start=(kk == 0), stop=(kk == 7))
                    last.then_inc(sems["c_v"], 1)
            bf = sg - 3
            bo = sg - 2

            def f_tile(t):
                if t % 2 == 1:
                    W(te, "c_ve", 4 * min(sg + 1, NS))
                W(te, f"s_3t{t % 4}", 16 * (t // 4 + 1))
                W(te, "c_oadd", t - 1)
                b0, b1 = (4, 5) if t % 2 == 0 else (6, 7)
                for kk in range(8):
                    te.matmul(ps[b0][:], ln3T[t % 4][:, kk, :],
                              m_sb[:, kk, 0:512],
                              start=(kk == 0), stop=False)
                    te.matmul(ps[b1][:], ln3T[t % 4][:, kk, :],
                              m_sb[:, kk, 512:1024],
                              start=(kk == 0), stop=False)
                te.matmul(ps[b0][:], tbl["c_ones"][:], biasT[:, 0:512],
                          start=False, stop=True)
                te.matmul(ps[b1][:], tbl["c_ones"][:], biasT[:, 512:1024],
                          start=False, stop=True).then_inc(sems["c_ff"], 1)

            def w_tile(t):
                W(te, f"s_at{t % 4}", 16 * (t // 4 + 1))
                W(te, "c_l2a", t - 1)
                b0, b1 = (0, 1) if t % 2 == 0 else (2, 3)
                last = None
                for kk in range(8):
                    te.matmul(ps[b0][:], aoT[t % AOT_RING][:, kk, :],
                              wo_sb[:, kk, 0:512],
                              start=(kk == 0), stop=(kk == 7))
                    last = te.matmul(ps[b1][:], aoT[t % AOT_RING][:, kk, :],
                                     wo_sb[:, kk, 512:1024],
                                     start=(kk == 0), stop=(kk == 7))
                last.then_inc(sems["c_wo"], 1)

            if 0 <= bf < NS:
                W(te, "s_mld", 16)
                W(te, "s_bias", 16)
                W(te, "c_dv", 4 * min(sg - 1, NS))
                W(te, "c_rope", min(8 * sg + 6, 8 * NS))
                f_tile(4 * bf)
                f_tile(4 * bf + 1)
            if 0 <= bo < NS:
                W(te, "s_wo", 16)
                W(te, "c_exp", 32 * min(sg - 1, NS))
                W(te, "c_rope", 8 * min(sg + 1, NS))
                w_tile(4 * bo)
                w_tile(4 * bo + 1)
            if 0 <= bf < NS:
                f_tile(4 * bf + 2)
                f_tile(4 * bf + 3)
            if 0 <= bo < NS:
                w_tile(4 * bo + 2)
                w_tile(4 * bo + 3)
            b = sg - 1
            if 0 <= b < NS:
                W(te, "c_rope", 8 * min(sg + 1, NS))
                W(te, "c_ve", 4 * min(sg + 1, NS))
                W(te, "c_rope", 8 * sg)
                W(te, "c_klo", sg)
                W(te, "c_vprev", b)
                W(te, "c_l2a", 4 * (sg - 1))
                W(te, "c_oadd", 4 * (sg - 2))
                for w in range(4 * b, 4 * b + 4):
                    sl = w - 4 * b
                    for r in range(2):
                        gr = 2 * w + r
                        W(te, "c_exp", 8 * w + 4 * r)
                        first = [True] * 4
                        for gq in range(2):
                            for half in range(2):
                                if w == 0 and half == 0:
                                    continue
                                if "sim" in KNO:
                                    continue
                                slot = 2 * gq + half
                                chunk = 2 * r + gq
                                ksrc = klo[b % 2] if half == 0 else khi[b % 2]
                                for fs in range(2):
                                    cc = chunk + 4 * fs
                                    for jj in range(4):
                                        te.matmul(
                                            ps[jj][:, 128 * slot:128 * (slot + 1)],
                                            ksrc[32 * jj:32 * (jj + 1), cc,
                                                 128 * sl:128 * (sl + 1)],
                                            qro[b % 2][32 * jj:32 * (jj + 1), cc,
                                                       128 * sl:128 * (sl + 1)],
                                            start=(fs == 0 and first[jj]),
                                            stop=False,
                                            tile_position=(32 * jj, 0))
                                for jj in range(4):
                                    first[jj] = False
                        for jj in range(4):
                            if "mask" in KNO or "sim" in KNO:
                                te.sem_inc(sems["c_sim"], 1)
                                continue
                            if w == 0:
                                te.matmul(ps[jj][:, 128:256],
                                          tbl["c_id"][:], tbl["c_m12"][:, 128:256],
                                          start=False, stop=False)
                                te.matmul(ps[jj][:, 384:512],
                                          tbl["c_id"][:], tbl["c_m12"][:, 384:512],
                                          start=False, stop=True
                                          ).then_inc(sems["c_sim"], 1)
                            else:
                                te.matmul(ps[jj][:], tbl["c_id"][:],
                                          tbl["c_m12"][:],
                                          start=False, stop=True
                                          ).then_inc(sems["c_sim"], 1)
                        W(te, "c_dv", w)
                        if "av" in KNO:
                            W(te, "c_exp", 8 * w + 4 * r + 4)
                            te.sem_inc(sems["c_avr"], 1)
                            continue
                        last = None
                        for hp in (0, 4, 1, 5, 2, 6, 3, 7):
                            if hp < 4:
                                W(te, "c_exp", 8 * w + 4 * r + hp + 1)
                            head = 8 * r + hp
                            bank = 4 + 2 * r + hp // 4
                            col0 = 65 * (hp % 4)
                            halves = (1,) if w == 0 else (0, 1)
                            for half in halves:
                                T = w - 1 + half
                                if T // 4 == b:
                                    rhsv = vpr[b % 2][:, T % 4,
                                                      65 * head:65 * head + 65]
                                else:
                                    rhsv = vprev[(b - 1) % 2][:,
                                                              65 * head:65 * head + 65]
                                last = te.matmul(
                                    ps[bank][:, col0:col0 + 65],
                                    abuf[gr % 2][:, hp % 4,
                                                 128 * (2 * (hp // 4) + half):
                                                 128 * (2 * (hp // 4) + half) + 128],
                                    rhsv,
                                    start=(half == halves[0]), stop=(half == 1))
                        last.then_inc(sems["c_avr"], 1)
    @block.vector
    def _(v):
        ins = None
        for rr in range(2):
            ones_ap = vpr[rr][:].rearrange("p s (h c) -> p (s h) c", c=65)[:, :, 64:65]
            ins = v.memset(ones_ap, 1.0)
        v.memset(cbias[:, 0:1], EPS)
        v.memset(cbias[:, 1:2], EXPB)
        DRI(v, "c_init")
        W(v, "s_cst", 16 * 6)
        for sg in range(-1, KLIM):
            if 0 <= sg < NS:
                if KDBG and sg == 0:
                    W(v, "s_dbg", 16)
                if KDBG and sg == 2:
                    W(v, "s_dbg", 96)
                for pair in range(8):
                    gi = 16 * sg + 2 * pair
                    W(v, "c_qk", gi + 2)
                    isq = pair < 4
                    dst = qro[sg % 2] if isq else khi[sg % 2]
                    cos = tbl["c_cosq"]
                    sin = tbl["c_sinq"]
                    cch = pair % 4
                    fa = ps[(2 * pair) % 6][:]
                    sa = ps[(2 * pair + 1) % 6][:]
                    cosb = cos[:].unsqueeze(1).to_broadcast([128, 4, 128])
                    sinb = sin[:].unsqueeze(1).to_broadcast([128, 4, 128])

                    def t4(ap):
                        return ap.rearrange("p (a n) -> p a n", n=128)
                    v.tensor_tensor(t4(dst[:, cch, :]), t4(fa), cosb,
                                    op=ALU.mult)
                    v.tensor_tensor(t4(dst[:, cch + 4, :]), t4(sa), cosb,
                                    op=ALU.mult)
                    v.tensor_tensor(t4(rs1[:]), t4(sa), sinb, op=ALU.mult)
                    v.tensor_tensor(t4(rs2[:]), t4(fa), sinb, op=ALU.mult)
                    DR(v)
                    v.tensor_sub(dst[:, cch, :], dst[:, cch, :], rs1[:])
                    v.tensor_add(dst[:, cch + 4, :], dst[:, cch + 4, :], rs2[:])
                    DRI(v, "c_rope")
                cosd = tbl["c_dl"][:, 0:1]
                sind = tbl["c_dl"][:, 1:2]
                for phase in range(2):
                    for pair in range(4):
                        for sl2 in range(4):
                            tgl = 4 * sg - 1 + sl2
                            if tgl < 0 or tgl >= NT:
                                continue
                            src = khi[(tgl // 4) % 2]
                            ssl = tgl % 4
                            f_ = src[:, pair, 128 * ssl:128 * (ssl + 1)]
                            s_ = src[:, pair + 4, 128 * ssl:128 * (ssl + 1)]
                            of = klo[sg % 2][:, pair, 128 * sl2:128 * (sl2 + 1)]
                            os_ = klo[sg % 2][:, pair + 4,
                                              128 * sl2:128 * (sl2 + 1)]
                            if phase == 0:
                                v.tensor_scalar(of, s_, sind, None, op0=ALU.mult)
                                v.tensor_scalar(os_, f_, sind, None, op0=ALU.mult)
                            else:
                                v.scalar_tensor_tensor(of, f_, cosd, of,
                                                       op0=ALU.mult, op1=ALU.add)
                                v.scalar_tensor_tensor(os_, s_, cosd, os_,
                                                       op0=ALU.mult,
                                                       op1=ALU.subtract)
                    DR(v)
                DRI(v, "c_klo")
            s1 = sg + 1
            if 0 <= s1 < NS:
                for hf in range(2):
                    p1 = 2 * s1 + hf
                    stb = st1[p1 % 2]
                    for i2 in range(2):
                        t = 4 * s1 + 2 * hf + i2
                        W(v, f"s_x{t % 2}", 16 * (t // 2 + 1))
                        xx = x_sb[t % X_RING][:]
                        v.bn_stats(stb[:, i2, 0:2, :].rearrange("p a b -> p (a b)"),
                                   xx[:, 0:512])
                        v.bn_stats(stb[:, i2, 2:4, :].rearrange("p a b -> p (a b)"),
                                   xx[:, 512:1024])
                    ln_aggr(v, stb, 2, ag1[p1 % 2]).then_inc(sems["c_ln1s"], 1)
                    W(v, "c_r1", p1 + 1)
                    v.reciprocal(ag1[p1 % 2][:, 2:4], ag1[p1 % 2][:, 2:4])
                    DR(v)
                    v.tensor_mul(ag1[p1 % 2][:, 4:6], ag1[p1 % 2][:, 4:6], ag1[p1 % 2][:, 2:4])
                    DRI(v, "c_nmr")
            bf = sg - 3
            bo = sg - 2
            for hf in range(2):
                if 0 <= bf < NS:
                    pairt = [4 * bf + 2 * hf, 4 * bf + 2 * hf + 1]
                    for i2, t in enumerate(pairt):
                        W(v, "c_ff", t + 1)
                        pf = (ps[4], ps[5]) if t % 2 == 0 else (ps[6], ps[7])
                        v.bn_stats(st4[:, i2, 0:2, :].rearrange("p a b -> p (a b)"),
                                   pf[0][:])
                        v.bn_stats(st4[:, i2, 2:4, :].rearrange("p a b -> p (a b)"),
                                   pf[1][:])
                    ln_aggr(v, st4, 2, ag4).then_inc(sems["c_l4g"], 1)
                if 0 <= bo < NS:
                    pairw = [4 * bo + 2 * hf, 4 * bo + 2 * hf + 1]
                    for i2, t in enumerate(pairw):
                        W(v, "c_wo", t + 1)
                        pw = (ps[0], ps[1]) if t % 2 == 0 else (ps[2], ps[3])
                        v.bn_stats(st2[:, i2, 0:2, :].rearrange("p a b -> p (a b)"),
                                   pw[0][:])
                        v.bn_stats(st2[:, i2, 2:4, :].rearrange("p a b -> p (a b)"),
                                   pw[1][:])
                    ln_aggr(v, st2, 2, ag2).then_inc(sems["c_l2g"], 1)
                if 0 <= bf < NS:
                    W(v, "c_r4", 2 * bf + hf + 1)
                    v.reciprocal(ag4[:, 2:4], ag4[:, 2:4])
                    DR(v)
                    v.tensor_mul(ag4[:, 4:6], ag4[:, 4:6], ag4[:, 2:4])
                    DR(v)
                    for i2, t in enumerate(pairt):
                        pf = (ps[4], ps[5]) if t % 2 == 0 else (ps[6], ps[7])
                        rstd = ag4[:, 2 + i2:3 + i2]
                        nmr = ag4[:, 4 + i2:5 + i2]
                        v.scalar_tensor_tensor(x1[t % 4][:, 0:512], pf[0][:],
                                               rstd, x1[t % 4][:, 0:512],
                                               op0=ALU.mult, op1=ALU.add)
                        v.scalar_tensor_tensor(x1[t % 4][:, 512:1024], pf[1][:],
                                               rstd, x1[t % 4][:, 512:1024],
                                               op0=ALU.mult, op1=ALU.add)
                        DR(v)
                        v.tensor_scalar(x1[t % 4][:, 0:512],
                                        x1[t % 4][:, 0:512], nmr, None,
                                        op0=ALU.add)
                        v.tensor_scalar(x1[t % 4][:, 512:1024],
                                        x1[t % 4][:, 512:1024], nmr, None,
                                        op0=ALU.add)
                        DRI(v, "c_oadd")
                if 0 <= bo < NS:
                    W(v, "c_r2", 2 * bo + hf + 1)
                    v.reciprocal(ag2[:, 2:4], ag2[:, 2:4])
                    DR(v)
                    v.tensor_mul(ag2[:, 4:6], ag2[:, 4:6], ag2[:, 2:4])
                    DRI(v, "c_n2")
                    for i2, t in enumerate(pairw):
                        W(v, "c_l2a", t + 1)
                        W(v, f"s_xr{t % 2}", 16 * (t // 2 + 1))
                        v.tensor_add(x1[t % 4][:], x1[t % 4][:],
                                     xr[t % XR_RING][:])
                        DRI(v, "c_x1")
            bo3 = sg - 2

            def ln3_stats(hf):
                if not (0 <= bo3 < NS):
                    return
                pairt = [4 * bo3 + 2 * hf, 4 * bo3 + 2 * hf + 1]
                W(v, "c_l3a", 4 * bo3 + 2 * hf)
                for i2, t in enumerate(pairt):
                    v.bn_stats(st3[:, i2, 0:2, :].rearrange("p a b -> p (a b)"),
                               x1[t % 4][:, 0:512])
                    v.bn_stats(st3[:, i2, 2:4, :].rearrange("p a b -> p (a b)"),
                               x1[t % 4][:, 512:1024])
                ln_aggr(v, st3, 2, ag3).then_inc(sems["c_l3g"], 1)

            def ln3_nmr(hf):
                if not (0 <= bo3 < NS):
                    return
                W(v, "c_r3", 2 * bo3 + hf + 1)
                v.reciprocal(ag3[:, 2:4], ag3[:, 2:4])
                DR(v)
                v.tensor_mul(ag3[:, 4:6], ag3[:, 4:6], ag3[:, 2:4])
                DR(v)
                pairt = [4 * bo3 + 2 * hf, 4 * bo3 + 2 * hf + 1]
                for i2, t in enumerate(pairt):
                    if t >= 2:
                        W(v, f"s_3t{(t - 2) % 4}", 16 * ((t - 2) // 4 + 1))
                    v.tensor_scalar(ln3bf[t % 2][:], x1[t % 4][:],
                                    ag3[:, 2 + i2:3 + i2],
                                    ag3[:, 4 + i2:5 + i2],
                                    op0=ALU.mult, op1=ALU.add)
                    DRI(v, "c_l3a")

            bb = sg - 1
            if not (0 <= bb < NS):
                ln3_stats(0)
                ln3_nmr(0)
                ln3_stats(1)
                ln3_nmr(1)
            if 0 <= bb < NS:
                for w in range(4 * bb, 4 * bb + 4):
                    if KDBG and w == 2:
                        W(v, "s_dbg", 96)
                    W(v, "c_avr", 2 * w + 2)
                    if w >= 2:
                        W(v, f"s_at{(w - 2) % 4}", 16 * ((w - 2) // 4 + 1))
                    if "div" in KNO:
                        v.sem_inc(sems["c_dv"], 1)
                        continue
                    for ab in range(4):
                        pden = ps[4 + ab][:, 0:260].rearrange(
                            "p (h c) -> p h c", c=65)
                        v.reciprocal(
                            rden[:, 4 * ab:4 * ab + 4],
                            pden[:, :, 64:65].rearrange("p a b -> p (a b)"))
                    DR(v)
                    for ab in range(4):
                        pden = ps[4 + ab][:, 0:260].rearrange(
                            "p (h c) -> p h c", c=65)
                        h0 = 8 * (ab // 2) + 4 * (ab % 2)
                        v.tensor_tensor(
                            ao[w % AO_RING][:, 64 * h0:64 * h0 + 256].rearrange(
                                "p (h c) -> p h c", c=64),
                            pden[:, :, 0:64],
                            rden[:, 4 * ab:4 * ab + 4].unsqueeze(2).to_broadcast(
                                [128, 4, 64]),
                            op=ALU.mult)
                    DRI(v, "c_dv")
                    if w == 4 * bb + 1:
                        ln3_stats(0)
                    elif w == 4 * bb + 2:
                        ln3_nmr(0)
                    elif w == 4 * bb + 3:
                        ln3_stats(1)
            if 0 <= sg < NS:
                W(v, "c_ve", 4 * sg + 4)
                W(v, "c_avr", 8 * sg - 6)
                v.tensor_copy(vprev[sg % 2][:], vpr[sg % 2][:, 3, :])
                DRI(v, "c_vprev")
            if 0 <= bb < NS:
                ln3_nmr(1)

    @block.scalar
    def _(a):
        W(a, "c_mbT", 1)
        a.activation(out=t12a[:, 0:512], in_=ps[0][:], func=AF.Copy)
        a.activation(out=t12a[:, 512:1024], in_=ps[1][:], func=AF.Copy)
        DRI(a, "c_mbe")
        W(a, "c_mbT", 2)
        a.activation(out=t12b[:, 0:512], in_=ps[2][:], func=AF.Copy, scale=-1.0)
        a.activation(out=t12b[:, 512:1024], in_=ps[3][:], func=AF.Copy, scale=-1.0)
        DRI(a, "c_mbe")
        W(a, "c_mbM", 1)
        a.activation(out=mslice[:, 0:512], in_=ps[4][:], func=AF.Copy,
                     scale=1.0 / 1024.0)
        a.activation(out=mslice[:, 512:1024], in_=ps[5][:], func=AF.Copy,
                     scale=1.0 / 1024.0)
        DRI(a, "c_mbe")

        for sg in range(-1, KLIM):
            if 0 <= sg < NS:
                for tt in range(SUP):
                    W(a, "c_v", 4 * sg + tt + 1)
                    W(a, "c_init", 1)
                    vdst = vpr[sg % 2][:, tt, :].rearrange("p (h c) -> p h c", c=65)
                    a.activation(out=vdst[:, 0:8, 0:64], in_=ps[6][:].rearrange(
                        "p (h c) -> p h c", c=64), func=AF.Copy)
                    a.activation(out=vdst[:, 8:16, 0:64],
                                 in_=ps[7][:].rearrange(
                                     "p (h c) -> p h c", c=64), func=AF.Copy
                                 ).then_inc(sems["c_ve"], 1)
            s1 = sg + 1
            if 0 <= s1 < NS:
                for hf in range(2):
                    p1 = 2 * s1 + hf
                    ag = ag1[p1 % 2]
                    W(a, "c_ln1s", p1 + 1)
                    a.activation(out=ag[:, 2:4], in_=ag[:, 2:4],
                                 func=AF.Sqrt)
                    DRI(a, "c_r1")
                    W(a, "c_nmr", p1 + 1)
                    for i2 in range(2):
                        t = 4 * s1 + 2 * hf + i2
                        a.activation(out=x_sb[t % X_RING][:],
                                     in_=x_sb[t % X_RING][:],
                                     func=AF.Identity, scale=ag[:, 2 + i2:3 + i2],
                                     bias=ag[:, 4 + i2:5 + i2]
                                     ).then_inc(sems["c_l1a"], 1)
            bf = sg - 3
            bo = sg - 2
            for hf in range(2):
                if 0 <= bf < NS:
                    W(a, "c_l4g", 2 * bf + hf + 1)
                    a.activation(out=ag4[:, 2:4], in_=ag4[:, 2:4],
                                 func=AF.Sqrt)
                    DRI(a, "c_r4")
                if 0 <= bf < NS:
                    for t in (4 * bf + 2 * hf, 4 * bf + 2 * hf + 1):
                        W(a, "c_oadd", t + 1)
                        a.dma_start(out=yout[128 * t:128 * (t + 1), :],
                                    in_=x1[t % 4][:]
                                    ).then_inc(sems[f"s_out{t % 2}"], 16)
                if 0 <= bo < NS:
                    pairt = [4 * bo + 2 * hf, 4 * bo + 2 * hf + 1]
                    W(a, "c_l2g", 2 * bo + hf + 1)
                    a.activation(out=ag2[:, 2:4], in_=ag2[:, 2:4],
                                 func=AF.Sqrt)
                    DRI(a, "c_r2")
                    W(a, "c_n2", 2 * bo + hf + 1)
                    for i2, t in enumerate(pairt):
                        W(a, "c_oadd", t - 3)
                        W(a, f"s_out{t % 2}", 16 * (t // 2 - 1))
                        pw = (ps[0], ps[1]) if t % 2 == 0 else (ps[2], ps[3])
                        a.activation(out=x1[t % 4][:, 0:512], in_=pw[0][:],
                                     func=AF.Identity, scale=ag2[:, 2 + i2:3 + i2],
                                     bias=ag2[:, 4 + i2:5 + i2])
                        a.activation(out=x1[t % 4][:, 512:1024], in_=pw[1][:],
                                     func=AF.Identity, scale=ag2[:, 2 + i2:3 + i2],
                                     bias=ag2[:, 4 + i2:5 + i2]
                                     ).then_inc(sems["c_l2a"], 1)
            bo = sg - 2

            def act_sqrt3(hf):
                if not (0 <= bo < NS):
                    return
                W(a, "c_l3g", 2 * bo + hf + 1)
                a.activation(out=ag3[:, 2:4], in_=ag3[:, 2:4], func=AF.Sqrt)
                DRI(a, "c_r3")

            bb = sg - 1
            if 0 <= bb < NS:
                for w in range(4 * bb, 4 * bb + 4):
                    for r in range(2):
                        gr = 2 * w + r
                        if KDBG and gr == 2:
                            W(a, "s_dbg", 96)
                        for sb in range(4):
                            W(a, "c_sim", 8 * w + 4 * r + sb + 1)
                            W(a, "c_avr", gr - 1)
                            if "exp" in KNO:
                                a.sem_inc(sems["c_exp"], 1)
                                continue
                            if w == 0:
                                src = ps[sb][:].rearrange(
                                    "p (s n) -> p s n", n=128)[:, 1:4:2, :]
                                dst = abuf[gr % 2][:, sb, :].rearrange(
                                    "p (s n) -> p s n", n=128)[:, 1:4:2, :]
                            else:
                                src = ps[sb][:]
                                dst = abuf[gr % 2][:, sb, :]
                            a.activation(out=dst, in_=src, func=AF.Exp,
                                         scale=8.0, bias=cbias[:, 1:2]
                                         ).then_inc(sems["c_exp"], 1)
                    if w == 4 * bb + 1:
                        act_sqrt3(0)
            bo = sg - 2
            if 0 <= bo < NS and not (0 <= sg - 1 < NS):
                act_sqrt3(0)
            if 0 <= bo < NS:
                act_sqrt3(1)
    blkmgr.__exit__(None, None, None)
    for cm in reversed(_ctxs):
        cm.__exit__(None, None, None)
    return nc, SB_REM


def kernel(L0_attn_x, L0_attn_W_qkv, L0_attn_W_out,
           L0_ffn_w_freq_real, L0_ffn_w_freq_imag, L0_ffn_bias):
    x = np.asarray(L0_attn_x, dtype=np.float32)
    Wqkv = np.asarray(L0_attn_W_qkv, dtype=np.float32)
    Wout = np.ascontiguousarray(
        np.asarray(L0_attn_W_out, dtype=np.float32).astype(ml_dtypes.bfloat16))
    wfr = np.ascontiguousarray(
        np.asarray(L0_ffn_w_freq_real, dtype=np.float32).astype(ml_dtypes.bfloat16))
    wfi = np.ascontiguousarray(
        np.asarray(L0_ffn_w_freq_imag, dtype=np.float32).astype(ml_dtypes.bfloat16))
    fbias = np.ascontiguousarray(
        np.asarray(L0_ffn_bias, dtype=np.float32).reshape(1, D)
        .astype(ml_dtypes.bfloat16))

    perm = _quad_perm()
    subs = []
    for i in range(16):
        which = 0 if i < 8 else 1
        c = CHUNK_ORDER[i % 8]
        cols = which * D + perm[c * 128:(c + 1) * 128]
        for kk in range(8):
            subs.append(np.ascontiguousarray(Wqkv[128 * kk:128 * (kk + 1), cols]))
    wqk_host = np.stack(subs, axis=0).astype(ml_dtypes.bfloat16)

    wv_host = np.ascontiguousarray(Wqkv[:, 2 * D:3 * D]).astype(ml_dtypes.bfloat16)
    cst = _host_constants()

    nc, _ = build_nc()

    grid = (2.0 * np.pi / D) * ((np.arange(D)[:, None] * np.arange(D)[None, :]) % D)
    Cd = np.cos(grid)
    Sd = np.sin(grid)

    in_maps = []
    for r in range(NCORE):
        dsub = np.ascontiguousarray(np.stack([
            Cd[:, 128 * r:128 * (r + 1)],
            Sd[:, 128 * r:128 * (r + 1)],
            -Sd[:, 128 * r:128 * (r + 1)],
        ]).astype(ml_dtypes.bfloat16))
        im = {
            "xin": np.ascontiguousarray(x[r]),
            "wqk": wqk_host,
            "wv": wv_host,
            "wo": Wout,
            "wfr": wfr,
            "wfi": wfi,
            "fbias": fbias,
            "dsub": dsub,
            "c_dftC": cst["c_dftC"],
            "c_dftS": cst["c_dftS"],
        }
        for nm in CONST_NAMES:
            im[nm] = cst[nm]
        in_maps.append(im)

    res = run_bass_kernel_spmd(nc, in_maps, core_ids=list(range(NCORE)))
    out = np.stack([res.results[r]["yout"] for r in range(NCORE)], axis=0)
    if os.environ.get("KDBG", "0") == "1":
        kernel.dbg_results = res.results
    return out
